# version 02 (final)
# baseline (speedup 1.0000x reference)
import math
import numpy as np
import concourse.bass as bass
import concourse.mybir as mybir
from concourse.bass_utils import run_bass_kernel_spmd

F32 = mybir.dt.float32
BF16 = mybir.dt.bfloat16
I32 = mybir.dt.int32
AF = mybir.ActivationFunctionType
ALU = mybir.AluOpType
AX = mybir.AxisListType

B, S, D = 4, 2048, 2048
HALF, NT, NTO, NC = S // 2, S // 128, S // 256, D // 128
HFFN, INTER, CONVBC = 8192, 4096, 256
TOPK, SELH, MHD, MH = 1024, 512, 128, 16
NEG = -30000.0
RSQ_D = 1.0 / math.sqrt(D)
RSQ_HD = 1.0 / math.sqrt(MHD)
RG_PAIRS = [[0, 1], [2, 3], [4, 5], [6, 7]]


def _causal_mask(parity):
    out = []
    for qi in range(8):
        gq = parity * 8 + qi
        kc = (qi + 9 + 3) // 4
        for j in range(kc):
            m = np.zeros((128, 512), np.float32)
            for kt in range(4):
                gk = 4 * j + kt
                sl = m[:, kt * 128:(kt + 1) * 128]
                if gk > gq:
                    sl[:] = NEG
                elif gk == gq:
                    r = np.arange(128)
                    sl[:] = np.where(r[:, None] >= r[None, :], 0.0, NEG)
            out.append(m)
    return np.concatenate(out, axis=0)


def _rope_tables():
    inv = 1.0 / (10000.0 ** (np.arange(0, D, 2, dtype=np.float64) / D))
    ang = np.arange(S, dtype=np.float64)[:, None] * inv[None, :]
    return np.cos(ang).astype(np.float32), np.sin(ang).astype(np.float32)


def stage_inputs(inp):
    cosT, sinT = _rope_tables()
    tris = np.tril(np.ones((128, 128), np.float32), -1)
    eye = np.eye(128, dtype=np.float32)
    triu = np.triu(np.ones((128, 128), np.float32))
    bigmask = (np.arange(4096)[None, :] <
               (np.arange(128)[:, None] + 2048)).astype(np.float32)
    cm = {0: _causal_mask(0), 1: _causal_mask(1)}
    shards = []
    for s in range(8):
        pair, half = s // 2, s % 2
        r0 = half * HALF
        d = {
            "x0": inp["L0_attn_q"][pair],
            "x0own": inp["L0_attn_q"][pair][r0:r0 + HALF],
            "wq": inp["L0_attn_W_q"], "wk": inp["L0_attn_W_k"],
            "wv": inp["L0_attn_W_v"], "wo": inp["L0_attn_W_o"],
            "cosT": cosT, "sinT": sinT,
            "cosQ": cosT[r0:r0 + HALF], "sinQ": sinT[r0:r0 + HALF],
            "cmask": cm[half],
            "eye": eye, "tris": tris, "triu": triu, "bigmask": bigmask,
            "wgT": inp["L0_ffn_W_g"].T, "wuT": inp["L0_ffn_W_u"].T,
            "wdT": inp["L0_ffn_W_d"].T,
            "wz": inp["L1_attn_W_z"][:, half * D:(half + 1) * D],
            "wxs": inp["L1_attn_W_xBC"][:, half * D:(half + 1) * D],
            "wbc": inp["L1_attn_W_xBC"][:, INTER:INTER + CONVBC],
            "wdt": np.concatenate(
                [inp["L1_attn_W_dt"][:, half * 32:(half + 1) * 32],
                 inp["L1_attn_W_dt"][:, (1 - half) * 32:(2 - half) * 32]], 1),
            "dtbias": np.concatenate(
                [inp["L1_attn_dt_bias"][half * 32:(half + 1) * 32],
                 inp["L1_attn_dt_bias"][(1 - half) * 32:(2 - half) * 32]]
            ).reshape(64, 1),
            "alog": inp["L1_attn_A_log"][half * 32:half * 32 + 32].reshape(32, 1),
            "dvec": inp["L1_attn_D"][half * 32:half * 32 + 32].reshape(1, 32),
            "cwx": np.concatenate([inp["L1_attn_conv_w"][half * D:(half + 1) * D],
                                   inp["L1_attn_conv_w"][INTER:INTER + CONVBC]], 0),
            "cbx": np.concatenate([inp["L1_attn_conv_b"][half * D:(half + 1) * D],
                                   inp["L1_attn_conv_b"][INTER:INTER + CONVBC]], 0
                                  ).reshape(-1, 1),
            "normw": inp["L1_attn_norm_w"][half * D:(half + 1) * D].reshape(1, D),
            "wout": inp["L1_attn_W_out"][half * D:(half + 1) * D, :],
            "wrouter": inp["L1_ffn_w_router"].reshape(1, D),
            "g1": inp["L1_ffn_g1"].reshape(1, D),
            "g2": inp["L1_ffn_g2"].reshape(1, D),
            "wq2": inp["L1_ffn_wq"], "wk2": inp["L1_ffn_wk"],
            "wv2": inp["L1_ffn_wv"], "wo2": inp["L1_ffn_wo"],
            "w1": inp["L1_ffn_w1"], "w3": inp["L1_ffn_w3"], "w2": inp["L1_ffn_w2"],
            "modmask": inp["L1_ffn_mask"][half * SELH:(half + 1) * SELH],
            "cosM4": np.tile(
                inp["L1_ffn_freqs_cis"][half * SELH:(half + 1) * SELH, :, 0],
                (1, 4)),
            "sinM4": np.tile(
                inp["L1_ffn_freqs_cis"][half * SELH:(half + 1) * SELH, :, 1],
                (1, 4)),
            "iota_own": np.tile(np.arange(half * SELH, (half + 1) * SELH,
                                          dtype=np.float32)[None, :], (128, 1)),
            "iota_posc": (np.concatenate([np.arange(half * 4, half * 4 + 4),
                                          np.arange((1 - half) * 4,
                                                    (1 - half) * 4 + 4)]
                                         )[None, :] * 128
                          + np.arange(128)[:, None]).astype(np.float32),
            "deltam": np.tile(np.eye(32, dtype=np.float32), (1, 16)
                              ).reshape(32, 16, 32).transpose(0, 1, 2
                              ).reshape(32, 512) * 0 + np.concatenate(
                [np.eye(32, dtype=np.float32)[:, None, :]] * 16, 1
                ).reshape(32, 512),
        }
        d = {k: np.ascontiguousarray(v, dtype=np.float32) for k, v in d.items()}
        shards.append(d)
    return shards


class Sems:
    def __init__(self, nc, names):
        self.h, self.c, self.cms = {}, {}, []
        for n in names:
            cm = nc.semaphore(n)
            self.h[n] = cm.__enter__()
            self.cms.append(cm)
            self.c[n] = 0

    def inc(self, ins, name, n=1):
        ins.then_inc(self.h[name], n)
        self.c[name] += n
        return self.c[name]

    def w(self, eng, name, val=None):
        eng.wait_ge(self.h[name], self.c[name] if val is None else val)


def build(nblocks=4, debug=False, b3cut=99):
    nc = bass.Bass("TRN2", target_bir_lowering=False, debug=False)
    E = {}

    def dram(name, shape, dtype, kind="ExternalInput", **kw):
        E[name] = nc.dram_tensor(name, list(shape), dtype, kind=kind, **kw).ap()
        return E[name]

    dram("x0", (S, D), F32)
    dram("x0own", (HALF, D), F32)
    for n in ("wq", "wk", "wv", "wo", "wz", "wxs", "wq2", "wk2", "wv2", "wo2"):
        dram(n, (D, D), F32)
    for n in ("wgT", "wuT", "w1", "w3"):
        dram(n, (D, HFFN), F32)
    for n in ("wdT", "w2"):
        dram(n, (HFFN, D), F32)
    dram("cosT", (S, D // 2), F32); dram("sinT", (S, D // 2), F32)
    dram("cosQ", (HALF, D // 2), F32); dram("sinQ", (HALF, D // 2), F32)
    dram("cmask", (28 * 128, 512), F32)
    dram("eye", (128, 128), F32); dram("tris", (128, 128), F32)
    dram("triu", (128, 128), F32); dram("bigmask", (128, 4096), F32)
    dram("wbc", (D, CONVBC), F32); dram("wdt", (D, 64), F32)
    dram("dtbias", (64, 1), F32); dram("alog", (32, 1), F32)
    dram("dvec", (1, 32), F32)
    dram("cwx", (D + CONVBC, 4), F32); dram("cbx", (D + CONVBC, 1), F32)
    dram("normw", (1, D), F32); dram("wout", (D, D), F32)
    dram("wrouter", (1, D), F32); dram("g1", (1, D), F32); dram("g2", (1, D), F32)
    dram("modmask", (SELH, TOPK), F32)
    dram("cosM4", (SELH, 256), F32); dram("sinM4", (SELH, 256), F32)
    dram("iota_own", (128, SELH), F32); dram("iota_posc", (128, 8), F32)
    dram("deltam", (32, 512), F32)

    spk = "ExternalOutput" if debug else "Internal"
    dram("q_sp", (HALF, D), BF16, kind=spk)
    dram("k_sp", (S, D), BF16, kind=spk)
    dram("v_sp", (S, D), BF16, kind=spk)
    dram("attn_sp", (HALF, D), BF16, kind=spk)
    dram("x1_sp", (HALF, D), F32, kind="Internal")
    dram("x2_sp", (HALF, D), F32, kind=spk)
    dram("x3_sp", (HALF, D), F32, kind=spk)
    dram("pT_sp", (HFFN, HALF), BF16, kind=spk)
    dram("wdb_sp", (HFFN, D), BF16, kind=spk)
    dram("gate_sp", (S, D), BF16, kind=spk)
    dram("cc1_in", (HALF, D), BF16, kind="Internal")
    dram("cc1_out", (S, D), BF16, kind="Internal", addr_space="Local")
    dram("cc2_in", (S, D + 128), BF16, kind="Internal")
    dram("cc2_out", (HALF, D + 128), BF16, kind="Internal", addr_space="Local")
    dram("cc3_in", (HALF + 2, D), BF16, kind="Internal")
    dram("cc3_out", (S + 4, D), BF16, kind="Internal", addr_space="Local")
    dram("cc4_in", (TOPK, D), BF16, kind="Internal")
    dram("cc4_out", (2 * TOPK, D), BF16, kind="Internal", addr_space="Local")
    dram("cc5_in", (S, D), BF16, kind="Internal")
    dram("cc5_out", (HALF, D), BF16, kind="Internal", addr_space="Local")
    dram("out", (HALF, D), F32, kind="ExternalOutput")
    if debug:
        dram("x1raw_sp", (HALF, D), F32, kind="ExternalOutput")
        dram("xq_sp", (HALF, D), BF16, kind="ExternalOutput")
        dram("dbg_xssm", (S, D), BF16, kind="ExternalOutput")
        dram("dbg_y", (S, D), BF16, kind="ExternalOutput")
        dram("dbg_bc", (256, S), BF16, kind="ExternalOutput")
        dram("dbg_dt", (64, S), F32, kind="ExternalOutput")
        dram("dbg_cc2", (HALF, D + 128), BF16, kind="ExternalOutput")

    ctxs = []

    def sbuf(name, shape, dtype):
        cm = nc.sbuf_tensor(name, list(shape), dtype)
        t = cm.__enter__(); ctxs.append(cm); return t

    def psum(name, shape, dtype):
        cm = nc.psum_tensor(name, list(shape), dtype)
        t = cm.__enter__(); ctxs.append(cm); return t

    big0 = sbuf("big0", (128, 32768), BF16)
    big1 = sbuf("big1", (128, 32768), BF16)
    big2 = sbuf("big2", (128, 8192), BF16)
    acc32 = sbuf("acc32", (128, 4096), F32)
    xt32 = sbuf("xt32", (128, 2048), F32)
    st32 = sbuf("st32", (128, 2048), F32)
    wf32 = sbuf("wf32", (128, 2048), F32)
    itile = sbuf("itile", (128, 1024), I32)
    stg = sbuf("stg", (128, 2048), BF16)
    stg2 = sbuf("stg2", (128, 2048), BF16)
    sm = sbuf("sm", (128, 64), F32)
    smb = sbuf("smb", (128, 64), F32)
    identb = sbuf("identb", (128, 128), BF16)
    identf = sbuf("identf", (128, 128), F32)
    onesb = sbuf("onesb", (1, 128), BF16)
    onescol = sbuf("onescol", (128, 1), BF16)
    onesf = sbuf("onesf", (128, 128), F32)
    trisb = sbuf("trisb", (128, 128), BF16)
    triuf = sbuf("triuf", (128, 128), F32)
    cosb = sbuf("cosb", (128, 256), F32)
    sinb = sbuf("sinb", (128, 256), F32)
    mskb = sbuf("mskb", (128, 512), F32)

    srow = sbuf("srow", (1, 576), F32)
    srowb = sbuf("srowb", (1, 576), BF16)
    pb = [psum(f"pb{i}", (128, 512), F32) for i in range(7)]
    ptb = psum("ptb", (128, 1024), BF16)

    sems = Sems(nc, ["dS", "dG", "dC", "tS", "vS", "sS", "gS", "ccS"])
    blk = nc.Block(); block = blk.__enter__()

    SY, GP, VE, SC, TE = (block.sync, block.gpsimd, block.vector,
                          block.scalar, block.tensor)

    def dma(q, dst, src, sem="dS", slow=False):
        q.dma_start(out=dst, in_=src,
                    allow_slow_non_contiguous=slow).then_inc(sems.h[sem], 16)
        sems.c[sem] += 16

    def vnop(v):
        iv = v.tensor_scalar_add(sm[:, 63:64], sm[:, 63:64], 0.0)
        sems.inc(iv, "vS")

    def transposes(srcs, dsts, wait_first=None):
        n = len(srcs)
        for g0 in range(0, n, 8):
            cnt = min(8, n - g0)

            def tseg(t, g0=g0, cnt=cnt, wf=(wait_first if g0 == 0 else None)):
                if wf is not None:
                    wf(t)
                if g0 > 0:
                    sems.w(t, "vS")
                for j in range(cnt):
                    ii = t.transpose(ptb[:, j * 128:(j + 1) * 128],
                                     srcs[g0 + j], identb[:, :])
                sems.inc(ii, "tS")
            TE(tseg)

            def vseg(v, g0=g0, cnt=cnt):
                sems.w(v, "tS")
                for j in range(cnt):
                    iv = v.tensor_copy(out=dsts[g0 + j],
                                       in_=ptb[:, j * 128:(j + 1) * 128])
                sems.inc(iv, "vS")
            VE(vseg)

    def ln(x_ap, Dw, eps, out_ap, sub_mean=True, scratch=None, g_ap=None,
           out_bf=None):
        scr = scratch if scratch is not None else acc32[:, 0:Dw]

        def v1(v):
            i0 = v.scalar_tensor_tensor(out=scr, in0=x_ap, scalar=1.0, in1=x_ap,
                                        op0=ALU.mult, op1=ALU.mult,
                                        accum_out=sm[:, 0:1])
            if sub_mean:
                i = v.reduce_sum(out=sm[:, 1:2], in_=x_ap, axis=AX.X)
                c = sems.inc(i, "vS"); sems.w(v, "vS", c)
                i = v.tensor_scalar_mul(sm[:, 2:3], sm[:, 1:2], 1.0 / Dw)
                c = sems.inc(i, "vS"); sems.w(v, "vS", c)
                i = v.tensor_tensor(out=sm[:, 3:4], in0=sm[:, 2:3],
                                    in1=sm[:, 2:3], op=ALU.mult)
                c = sems.inc(i, "vS"); sems.w(v, "vS", c)
                i = v.scalar_tensor_tensor(out=sm[:, 3:4], in0=sm[:, 0:1],
                                           scalar=1.0 / Dw, in1=sm[:, 3:4],
                                           op0=ALU.mult, op1=ALU.subtract)
                c = sems.inc(i, "vS"); sems.w(v, "vS", c)
            else:
                c = sems.inc(i0, "vS"); sems.w(v, "vS", c)
                i = v.tensor_scalar_mul(sm[:, 3:4], sm[:, 0:1], 1.0 / Dw)
                c = sems.inc(i, "vS"); sems.w(v, "vS", c)
            iv = v.tensor_scalar_add(sm[:, 3:4], sm[:, 3:4], eps)
            sems.inc(iv, "vS")
        VE(v1)

        def s1(s):
            sems.w(s, "vS")
            i = s.activation(out=sm[:, 4:5], in_=sm[:, 3:4], func=AF.Ln)
            c = sems.inc(i, "sS"); sems.w(s, "sS", c)
            ii = s.activation(out=sm[:, 4:5], in_=sm[:, 4:5], func=AF.Exp,
                              scale=-0.5)
            sems.inc(ii, "sS")
        SC(s1)

        def v2(v):
            sems.w(v, "sS")
            if sub_mean:
                iv = v.tensor_scalar(out=out_ap, in0=x_ap, scalar1=sm[:, 2:3],
                                     scalar2=sm[:, 4:5], op0=ALU.subtract,
                                     op1=ALU.mult)
            else:
                iv = v.tensor_scalar(out=out_ap, in0=x_ap, scalar1=sm[:, 4:5],
                                     scalar2=None, op0=ALU.mult)
            if g_ap is not None:
                c = sems.inc(iv, "vS"); sems.w(v, "vS", c)
                iv = v.tensor_tensor(out=out_ap, in0=out_ap, in1=g_ap,
                                     op=ALU.mult)
            if out_bf is not None:
                c = sems.inc(iv, "vS"); sems.w(v, "vS", c)
                iv = v.tensor_copy(out=out_bf, in_=out_ap)
            sems.inc(iv, "vS")
        VE(v2)

    def p_const():
        def f(q):
            dma(q, identf[:, :], E["eye"][:, :])
            dma(q, triuf[:, :], E["triu"][:, :])
        SY(f)

        def g(gq):
            dma(gq, identb[:, :], E["eye"][:, :], "dG")
            dma(gq, trisb[:, :], E["tris"][:, :], "dG")
        GP(g)

        def v(v):
            sems.w(v, "dS"); sems.w(v, "dG")
            v.memset(onesb[:], 1.0)
            v.memset(onescol[:], 1.0)
            iv = v.memset(onesf[:], 1.0)
            sems.inc(iv, "vS")
        VE(v)

    def x_to_xT(src_getter, dst_fn, ntiles, tok_per_col=2048, cast=True):
        for tt in range(ntiles):
            def ld(q, tt=tt):
                sems.w(q, "vS")
                dma(q, xt32[:, :] if cast else stg[:, :], src_getter(tt))
            SY(ld)
            if cast:
                def cst(v):
                    sems.w(v, "dS")
                    iv = v.tensor_copy(out=stg[:, :], in_=xt32[:, :])
                    sems.inc(iv, "vS")
                VE(cst)
            else:
                def cst(v):
                    sems.w(v, "dS")
                    vnop(v)
                VE(cst)
            transposes([stg[:, c * 128:(c + 1) * 128] for c in range(NC)],
                       [dst_fn(c, tt) for c in range(NC)],
                       wait_first=lambda t: sems.w(t, "vS"))

    def stream_proj(wname, ntt, src_big, rope, cosname, spname,
                    wbuf_base=0):
        for dc in range(4):
            buf = big1[:, wbuf_base + (dc % 2) * 8192:
                       wbuf_base + (dc % 2) * 8192 + 8192]

            def ldw(g, wname=wname, dc=dc, buf=buf):
                sems.w(g, "tS")
                g.dma_start(
                    out=buf.rearrange("p (a d) -> p a d", a=16),
                    in_=E[wname][:, dc * 512:(dc + 1) * 512].rearrange(
                        "(a p) d -> p a d", p=128)
                ).then_inc(sems.h["dG"], 16)
                sems.c["dG"] += 16
            GP(ldw)
            for tt in range(ntt):
                def mm(t, tt=tt, buf=buf, dc=dc):
                    if tt == 0:
                        sems.w(t, "dG")
                    sems.w(t, "vS")
                    bank = pb[tt % 2]
                    for cg in range(16):
                        ii = t.matmul(bank[:, :], src_big(cg, tt),
                                      buf[:, cg * 512:(cg + 1) * 512],
                                      start=(cg == 0), stop=(cg == 15))
                    sems.inc(ii, "tS")
                TE(mm)
                if rope:
                    def ldcs(q, tt=tt, dc=dc):
                        sems.w(q, "vS")
                        dma(q, cosb[:, :],
                            E["cos" + cosname][tt * 128:(tt + 1) * 128,
                                               dc * 256:(dc + 1) * 256])
                        dma(q, sinb[:, :],
                            E["sin" + cosname][tt * 128:(tt + 1) * 128,
                                               dc * 256:(dc + 1) * 256])
                    SY(ldcs)

                    def ev(v, tt=tt):
                        bank = pb[tt % 2]
                        sems.w(v, "tS"); sems.w(v, "dS")
                        ev_, od = bank[:, 0:512:2], bank[:, 1:512:2]
                        v.tensor_tensor(out=xt32[:, 0:256], in0=ev_,
                                        in1=cosb[:, :], op=ALU.mult)
                        v.tensor_tensor(out=xt32[:, 256:512], in0=od,
                                        in1=sinb[:, :], op=ALU.mult)
                        v.tensor_tensor(out=xt32[:, 512:768], in0=ev_,
                                        in1=sinb[:, :], op=ALU.mult)
                        i4 = v.tensor_tensor(out=xt32[:, 768:1024], in0=od,
                                             in1=cosb[:, :], op=ALU.mult)
                        c4 = sems.inc(i4, "vS"); sems.w(v, "vS", c4)
                        v.tensor_tensor(out=stg2[:, 0:512:2],
                                        in0=xt32[:, 0:256],
                                        in1=xt32[:, 256:512],
                                        op=ALU.subtract)
                        iv = v.tensor_tensor(out=stg2[:, 1:512:2],
                                             in0=xt32[:, 512:768],
                                             in1=xt32[:, 768:1024],
                                             op=ALU.add)
                        sems.inc(iv, "vS")
                    VE(ev)
                else:
                    def ev(v, tt=tt):
                        bank = pb[tt % 2]
                        sems.w(v, "tS")
                        iv = v.tensor_copy(out=stg2[:, 0:512],
                                           in_=bank[:, :])
                        sems.inc(iv, "vS")
                    VE(ev)

                def stq(q, spname=spname, tt=tt, dc=dc):
                    sems.w(q, "vS")
                    dma(q, E[spname][tt * 128:(tt + 1) * 128,
                                     dc * 512:(dc + 1) * 512],
                        stg2[:, 0:512])
                SY(stq)

                def wst(v):
                    sems.w(v, "dS")
                    vnop(v)
                VE(wst)

    def p_block1():
        x_to_xT(lambda tt: E["x0own"][tt * 128:(tt + 1) * 128, :],
                lambda c, tt: big0[:, c * 1024 + tt * 128:
                                   c * 1024 + (tt + 1) * 128], NTO)
        stream_proj("wq", NTO,
                    lambda c, tt: big0[:, c * 1024 + tt * 128:
                                       c * 1024 + (tt + 1) * 128],
                    True, "Q", "q_sp")
        x_to_xT(lambda tt: E["x0"][tt * 128:(tt + 1) * 128, :],
                lambda c, tt: big0[:, c * 2048 + tt * 128:
                                   c * 2048 + (tt + 1) * 128], NT)
        stream_proj("wk", NT,
                    lambda c, tt: big0[:, c * 2048 + tt * 128:
                                       c * 2048 + (tt + 1) * 128],
                    True, "T", "k_sp")
        stream_proj("wv", NT,
                    lambda c, tt: big0[:, c * 2048 + tt * 128:
                                       c * 2048 + (tt + 1) * 128],
                    False, "T", "v_sp")

        x_to_xT(lambda tt: E["k_sp"][tt * 128:(tt + 1) * 128, :],
                lambda c, tt: big0[:, c * 2048 + tt * 128:
                                   c * 2048 + (tt + 1) * 128], NT, cast=False)
        for tt in range(NT):
            def ldv(q, tt=tt):
                if tt == 0:
                    sems.w(q, "tS")
                dma(q, big1[:, tt * 2048:(tt + 1) * 2048],
                    E["v_sp"][tt * 128:(tt + 1) * 128, :])
            SY(ldv)

        def QT(d):
            return big2[:, d * 128:(d + 1) * 128]

        def PT(kt):
            return big2[:, 2048 + kt * 128:2048 + (kt + 1) * 128]

        for qi in range(NTO):
            kc = (qi + 9 + 3) // 4
            nkt = kc * 4
            moff = sum((q2 + 9 + 3) // 4 for q2 in range(qi))

            def ldq(q, qi=qi):
                sems.w(q, "tS"); sems.w(q, "vS")
                dma(q, stg2[:, :], E["q_sp"][qi * 128:(qi + 1) * 128, :])
            SY(ldq)

            def wq_(v):
                sems.w(v, "dS")
                vnop(v)
            VE(wq_)
            transposes([stg2[:, c * 128:(c + 1) * 128] for c in range(NC)],
                       [QT(c) for c in range(NC)],
                       wait_first=lambda t: sems.w(t, "vS"))

            for j in range(kc):
                def mm_s(t, j=j):
                    sems.w(t, "vS")
                    bank = pb[2 + (j % 2)]
                    for d2 in range(NC):
                        ii = t.matmul(bank[:, :], QT(d2),
                                      big0[:, d2 * 2048 + j * 512:
                                           d2 * 2048 + (j + 1) * 512],
                                      start=(d2 == 0), stop=(d2 == NC - 1))
                    sems.inc(ii, "tS")
                TE(mm_s)

                def ldm(q, j=j, moff=moff):
                    sems.w(q, "vS")
                    dma(q, mskb[:, :],
                        E["cmask"][(moff + j) * 128:(moff + j + 1) * 128, :])
                SY(ldm)

                def ev_s(v, j=j):
                    bank = pb[2 + (j % 2)]
                    sems.w(v, "tS"); sems.w(v, "dS")
                    iv = v.scalar_tensor_tensor(
                        out=st32[:, j * 512:(j + 1) * 512], in0=bank[:, :],
                        scalar=RSQ_D, in1=mskb[:, :], op0=ALU.mult, op1=ALU.add)
                    sems.inc(iv, "vS")
                VE(ev_s)

            def smx_v(v, kc=kc):
                sems.w(v, "vS")
                iv = v.reduce_max(out=sm[:, 8:9], in_=st32[:, 0:kc * 512],
                                  axis=AX.X, negate=True)
                sems.inc(iv, "vS")
            VE(smx_v)

            def smx_s(s, kc=kc):
                sems.w(s, "vS")
                ii = s.activation(out=stg[:, 0:kc * 512],
                                  in_=st32[:, 0:kc * 512], func=AF.Exp,
                                  bias=sm[:, 8:9], accum_out=sm[:, 9:10])
                sems.inc(ii, "sS")
            SC(smx_s)

            def smx_r(v):
                sems.w(v, "sS")
                iv = v.reciprocal(out=sm[:, 10:11], in_=sm[:, 9:10])
                sems.inc(iv, "vS")
            VE(smx_r)
            transposes([stg[:, k * 128:(k + 1) * 128] for k in range(nkt)],
                       [PT(k) for k in range(nkt)],
                       wait_first=lambda t: sems.w(t, "sS"))

            for ec in range(4):
                def mm_av(t, ec=ec, nkt=nkt):
                    sems.w(t, "vS")
                    bank = pb[4 + (ec % 2)]
                    for kt in range(nkt):
                        ii = t.matmul(bank[:, :], PT(kt),
                                      big1[:, kt * 2048 + ec * 512:
                                           kt * 2048 + ec * 512 + 512],
                                      start=(kt == 0), stop=(kt == nkt - 1))
                    sems.inc(ii, "tS")
                TE(mm_av)

                def ev_av(v, ec=ec):
                    bank = pb[4 + (ec % 2)]
                    sems.w(v, "tS"); sems.w(v, "vS")
                    iv = v.tensor_scalar(out=stg2[:, ec * 512:(ec + 1) * 512],
                                         in0=bank[:, :], scalar1=sm[:, 10:11],
                                         scalar2=None, op0=ALU.mult)
                    sems.inc(iv, "vS")
                VE(ev_av)

            def st_at(q, qi=qi):
                sems.w(q, "vS")
                dma(q, E["attn_sp"][qi * 128:(qi + 1) * 128, :], stg2[:, :])
            SY(st_at)

        for dc in range(4):
            def ldwo(g, dc=dc):
                if dc == 0:
                    sems.w(g, "tS")
                g.dma_start(
                    out=big1[:, dc * 8192:(dc + 1) * 8192].rearrange(
                        "p (a d) -> p a d", a=16),
                    in_=E["wo"][:, dc * 512:(dc + 1) * 512].rearrange(
                        "(a p) d -> p a d", p=128)
                ).then_inc(sems.h["dG"], 16)
                sems.c["dG"] += 16
            GP(ldwo)

        def AT(e):
            return big2[:, 4096 + e * 128:4096 + (e + 1) * 128]

        for tt in range(NTO):
            def lda(q, tt=tt):
                sems.w(q, "tS"); sems.w(q, "vS")
                dma(q, stg2[:, :], E["attn_sp"][tt * 128:(tt + 1) * 128, :])
            SY(lda)

            def wa(v):
                sems.w(v, "dS")
                vnop(v)
            VE(wa)
            transposes([stg2[:, c * 128:(c + 1) * 128] for c in range(NC)],
                       [AT(c) for c in range(NC)],
                       wait_first=lambda t: sems.w(t, "vS"))

            def ldx0(q, tt=tt):
                dma(q, xt32[:, :], E["x0own"][tt * 128:(tt + 1) * 128, :])
            SY(ldx0)

            for oc in range(4):
                def mm_o(t, oc=oc):
                    sems.w(t, "vS"); sems.w(t, "dG")
                    bank = pb[oc % 2]
                    for e in range(NC):
                        ii = t.matmul(bank[:, :], AT(e),
                                      big1[:, oc * 8192 + e * 512:
                                           oc * 8192 + (e + 1) * 512],
                                      start=(e == 0), stop=(e == NC - 1))
                    sems.inc(ii, "tS")
                TE(mm_o)

                def ev_o(v, oc=oc):
                    bank = pb[oc % 2]
                    sems.w(v, "tS"); sems.w(v, "dS")
                    iv = v.tensor_tensor(out=st32[:, oc * 512:(oc + 1) * 512],
                                         in0=bank[:, :],
                                         in1=xt32[:, oc * 512:(oc + 1) * 512],
                                         op=ALU.add)
                    sems.inc(iv, "vS")
                VE(ev_o)

            if debug:
                def stxr(q, tt=tt):
                    sems.w(q, "vS")
                    dma(q, E["x1raw_sp"][tt * 128:(tt + 1) * 128, :], st32[:, :])
                SY(stxr)

                def wxr(v):
                    sems.w(v, "dS")
                    vnop(v)
                VE(wxr)
            ln(st32[:, :], D, 1e-5, st32[:, :])

            def stx1(q, tt=tt):
                sems.w(q, "vS")
                dma(q, E["x1_sp"][tt * 128:(tt + 1) * 128, :], st32[:, :])
            SY(stx1)
            if nblocks >= 2:
                ln(st32[:, :], D, 1e-8, xt32[:, :])

                def qnt(v):
                    i = v.reduce_max(out=sm[:, 12:13], in_=xt32[:, :], axis=AX.X,
                                     apply_absolute_value=True)
                    c = sems.inc(i, "vS"); sems.w(v, "vS", c)
                    i = v.reciprocal(out=sm[:, 13:14], in_=sm[:, 12:13])
                    c = sems.inc(i, "vS"); sems.w(v, "vS", c)
                    i = v.tensor_scalar_mul(sm[:, 13:14], sm[:, 13:14], 127.0)
                    i = v.tensor_scalar_mul(sm[:, 14:15], sm[:, 12:13], 1.0 / 127.0)
                    c = sems.inc(i, "vS"); sems.w(v, "vS", c)
                    i = v.tensor_scalar(out=acc32[:, 0:2048], in0=xt32[:, :],
                                        scalar1=sm[:, 13:14], scalar2=None,
                                        op0=ALU.mult)
                    c = sems.inc(i, "vS"); sems.w(v, "vS", c)
                    i = v.tensor_copy(out=itile[:, :], in_=acc32[:, 0:1024])
                    c = sems.inc(i, "vS"); sems.w(v, "vS", c)
                    i = v.tensor_copy(out=acc32[:, 0:1024], in_=itile[:, :])
                    c = sems.inc(i, "vS"); sems.w(v, "vS", c)
                    i = v.tensor_copy(out=itile[:, :], in_=acc32[:, 1024:2048])
                    c = sems.inc(i, "vS"); sems.w(v, "vS", c)
                    i = v.tensor_copy(out=acc32[:, 1024:2048], in_=itile[:, :])
                    c = sems.inc(i, "vS"); sems.w(v, "vS", c)
                    i = v.tensor_scalar(out=acc32[:, 0:2048], in0=acc32[:, 0:2048],
                                        scalar1=sm[:, 14:15], scalar2=None,
                                        op0=ALU.mult)
                    c = sems.inc(i, "vS"); sems.w(v, "vS", c)
                    iv = v.tensor_copy(out=stg[:, :], in_=acc32[:, 0:2048])
                    sems.inc(iv, "vS")
                VE(qnt)
                if debug:
                    def stxq(q, tt=tt):
                        sems.w(q, "vS")
                        dma(q, E["xq_sp"][tt * 128:(tt + 1) * 128, :], stg[:, :])
                    SY(stxq)

                    def wxq(v):
                        sems.w(v, "dS")
                        vnop(v)
                    VE(wxq)
                transposes([stg[:, c * 128:(c + 1) * 128] for c in range(NC)],
                           [big0[:, 16384 + c * 1024 + tt * 128:
                                 16384 + c * 1024 + (tt + 1) * 128]
                            for c in range(NC)],
                           wait_first=lambda t: sems.w(t, "vS"))

    def p_block2():
        sgw = stg2
        sgd = lambda: big2[:, 1024:3072]
        pbuf = lambda: big2[:, 0:1024]
        for ht in range(HFFN // 128):
            for wi, wname in enumerate(("wgT", "wuT")):
                def ldw(g, wname=wname, ht=ht, wi=wi):
                    sems.w(g, "sS")
                    g.dma_start(
                        out=wf32.rearrange("p (a d) -> p a d", a=16),
                        in_=E[wname][:, ht * 128:(ht + 1) * 128].rearrange(
                            "(a p) d -> p a d", p=128)
                    ).then_inc(sems.h["dG"], 16)
                    sems.c["dG"] += 16
                GP(ldw)

                def sgn(s, wi=wi):
                    sems.w(s, "dG")
                    sems.w(s, "tS")
                    ii = s.activation(out=sgw[:, :], in_=wf32[:, :],
                                      func=AF.Sign)
                    sems.inc(ii, "sS")
                SC(sgn)

                def mm(t, wi=wi):
                    sems.w(t, "sS")
                    sems.w(t, "vS"); sems.w(t, "sS")
                    for tc in range(2):
                        bank = pb[wi * 2 + tc]
                        for cg in range(16):
                            ii = t.matmul(
                                bank[:, :], sgw[:, cg * 128:(cg + 1) * 128],
                                big0[:, 16384 + cg * 1024 + tc * 512:
                                     16384 + cg * 1024 + (tc + 1) * 512],
                                start=(cg == 0), stop=(cg == 15))
                    sems.inc(ii, "tS")
                TE(mm)
            def ldwd(q, ht=ht):
                sems.w(q, "sS")
                dma(q, xt32[:, :], E["wdT"][ht * 128:(ht + 1) * 128, :])
            SY(ldwd)

            def sgnd(s):
                sems.w(s, "dS"); sems.w(s, "dG")
                ii = s.activation(out=sgd()[:, :], in_=xt32[:, :],
                                  func=AF.Sign)
                sems.inc(ii, "sS")
            SC(sgnd)

            def stwd(g, ht=ht):
                sems.w(g, "sS")
                g.dma_start(out=E["wdb_sp"][ht * 128:(ht + 1) * 128, :],
                            in_=sgd()[:, :]).then_inc(sems.h["dG"], 16)
                sems.c["dG"] += 16
            GP(stwd)

            def evg(s):
                sems.w(s, "tS")
                s.activation(out=stg[:, 0:512], in_=pb[0][:, :], func=AF.Silu)
                ii = s.activation(out=stg[:, 512:1024], in_=pb[1][:, :],
                                  func=AF.Silu)
                sems.inc(ii, "sS")
            SC(evg)

            def evu(v):
                sems.w(v, "tS"); sems.w(v, "dS")
                v.tensor_copy(out=stg[:, 1024:1536], in_=pb[2][:, :])
                i = v.tensor_copy(out=stg[:, 1536:2048], in_=pb[3][:, :])
                c = sems.inc(i, "vS"); sems.w(v, "vS", c)
                sems.w(v, "sS")
                iv = v.tensor_tensor(out=pbuf()[:, :], in0=stg[:, 0:1024],
                                     in1=stg[:, 1024:2048], op=ALU.mult)
                sems.inc(iv, "vS")
            VE(evu)

            def stp(q, ht=ht):
                sems.w(q, "vS")
                dma(q, E["pT_sp"][ht * 128:(ht + 1) * 128, :], pbuf()[:, :])
            SY(stp)

        wdb_b = [big2[:, 3072 + i * 512:3072 + (i + 1) * 512] for i in range(2)]
        ptb_b = [big2[:, 4096 + i * 512:4096 + (i + 1) * 512] for i in range(2)]
        tb = [(xt32, 0), (st32, 0), (acc32, 0), (acc32, 2048)]

        def tbuf(tt, c0=0, c1=D):
            tns, base = tb[tt]
            return tns[:, base + c0:base + c1]
        mmct = [0, 0]
        for th in range(2):
            for dc in range(4):
                for ht in range(HFFN // 128):
                    par = ht % 2
                    guard = mmct[par]

                    def ldd(q, ht=ht, dc=dc, par=par, guard=guard):
                        sems.w(q, "tS", guard)
                        dma(q, wdb_b[par],
                            E["wdb_sp"][ht * 128:(ht + 1) * 128,
                                        dc * 512:(dc + 1) * 512])
                    SY(ldd)

                    def ldp(g, ht=ht, th=th, par=par, guard=guard):
                        sems.w(g, "tS", guard)
                        g.dma_start(out=ptb_b[par],
                                    in_=E["pT_sp"][ht * 128:(ht + 1) * 128,
                                                   th * 512:(th + 1) * 512]
                                    ).then_inc(sems.h["dG"], 16)
                        sems.c["dG"] += 16
                    GP(ldp)

                    def mm(t, ht=ht, par=par):
                        sems.w(t, "dS"); sems.w(t, "dG")
                        if ht == 0:
                            sems.w(t, "vS")
                        for tt in range(4):
                            ii = t.matmul(pb[tt][:, :],
                                          ptb_b[par][:, tt * 128:(tt + 1) * 128],
                                          wdb_b[par][:, :],
                                          start=(ht == 0),
                                          stop=(ht == HFFN // 128 - 1))
                        sems.inc(ii, "tS")
                    TE(mm)
                    mmct[par] = sems.c["tS"]

                def evd(v, dc=dc):
                    sems.w(v, "tS")
                    for tt in range(4):
                        iv = v.tensor_copy(
                            out=tbuf(tt, dc * 512, (dc + 1) * 512),
                            in_=pb[tt][:, :])
                    sems.inc(iv, "vS")
                VE(evd)
            for tt in range(4):
                row0 = th * 512 + tt * 128

                def ldx1(q, row0=row0):
                    sems.w(q, "vS")
                    dma(q, wf32[:, :], E["x1_sp"][row0:row0 + 128, :])
                SY(ldx1)

                def addx(v, tt=tt):
                    sems.w(v, "dS")
                    iv = v.tensor_tensor(out=tbuf(tt), in0=tbuf(tt),
                                         in1=wf32[:, :], op=ALU.add)
                    sems.inc(iv, "vS")
                VE(addx)
                ln(tbuf(tt), D, 1e-5, tbuf(tt), scratch=wf32[:, :],
                   out_bf=stg[:, :])

                def stx2(q, row0=row0, tt=tt):
                    sems.w(q, "vS")
                    dma(q, E["x2_sp"][row0:row0 + 128, :], tbuf(tt))
                    dma(q, E["cc1_in"][row0:row0 + 128, :], stg[:, :])
                SY(stx2)

                def wst(v):
                    sems.w(v, "dS")
                    vnop(v)
                VE(wst)

        def cc1(g):
            sems.w(g, "dS")
            ii = g.collective_compute(
                "AllGather", ALU.bypass, replica_groups=RG_PAIRS,
                ins=[E["cc1_in"][:, :]], outs=[E["cc1_out"][:, :]])
            sems.inc(ii, "ccS")
        GP(cc1)

    def p_block3(cut=99):
        NCH = 16

        def wcc(v):
            sems.w(v, "ccS")
            vnop(v)
        VE(wcc)
        x_to_xT(lambda tt: E["cc1_out"][tt * 128:(tt + 1) * 128, :],
                lambda c, tt: big0[:, c * 2048 + tt * 128:
                                   c * 2048 + (tt + 1) * 128], NT, cast=False)
        stream_proj("wz", NT,
                    lambda c, tt: big0[:, c * 2048 + tt * 128:
                                       c * 2048 + (tt + 1) * 128],
                    False, "T", "gate_sp")

        if cut <= 1:
            return
        def lddt(g):
            sems.w(g, "vS"); sems.w(g, "dS")
            g.dma_start(out=stg2[:, 0:1024].rearrange("p (a d) -> p a d", a=16),
                        in_=E["wdt"].rearrange("(a p) d -> p a d", p=128)
                        ).then_inc(sems.h["dG"], 16)
            sems.c["dG"] += 16
        GP(lddt)

        def lddtb(q):
            dma(q, smb[0:64, 0:1], E["dtbias"][:, :])
        SY(lddtb)
        for tc in range(4):
            def mmdt(t, tc=tc):
                sems.w(t, "dG")
                if tc > 0:
                    sems.w(t, "vS")
                for cg in range(16):
                    ii = t.matmul(pb[4][0:64, :],
                                  stg2[:, cg * 64:(cg + 1) * 64],
                                  big0[:, cg * 2048 + tc * 512:
                                       cg * 2048 + (tc + 1) * 512],
                                  start=(cg == 0), stop=(cg == 15))
                sems.inc(ii, "tS")
            TE(mmdt)

            def spv1(v, tc=tc):
                sems.w(v, "tS"); sems.w(v, "dS")
                i = v.tensor_scalar(out=wf32[0:64, 0:512], in0=pb[4][0:64, :],
                                    scalar1=smb[0:64, 0:1], scalar2=None,
                                    op0=ALU.add)
                c = sems.inc(i, "vS"); sems.w(v, "vS", c)
                sems.inc(v.tensor_scalar_add(sm[0:64, 20:21], sm[0:64, 20:21],
                                             0.0), "vS")
            VE(spv1)

            def sps1(s, tc=tc):
                sems.w(s, "vS")
                i = s.activation(out=wf32[0:64, 512:1024],
                                 in_=wf32[0:64, 0:512], func=AF.Abs)
                c = sems.inc(i, "sS"); sems.w(s, "sS", c)
                i = s.activation(out=wf32[0:64, 512:1024],
                                 in_=wf32[0:64, 512:1024], func=AF.Exp,
                                 scale=-1.0)
                c = sems.inc(i, "sS"); sems.w(s, "sS", c)
                i = s.activation(out=wf32[0:64, 512:1024],
                                 in_=wf32[0:64, 512:1024], func=AF.Ln,
                                 bias=1.0)
                sems.inc(i, "sS")
            SC(sps1)

            def spv2(v, tc=tc):
                sems.w(v, "sS")
                i = v.tensor_scalar_max(wf32[0:64, 1024:1536],
                                        wf32[0:64, 0:512], 0.0)
                c = sems.inc(i, "vS"); sems.w(v, "vS", c)
                i = v.tensor_tensor(out=wf32[0:64, 1024:1536],
                                    in0=wf32[0:64, 1024:1536],
                                    in1=wf32[0:64, 512:1024], op=ALU.add)
                c = sems.inc(i, "vS"); sems.w(v, "vS", c)
                i = v.tensor_scalar(out=wf32[0:64, 1536:2048],
                                    in0=wf32[0:64, 0:512], scalar1=10.0,
                                    scalar2=None, op0=ALU.is_lt)
                c = sems.inc(i, "vS"); sems.w(v, "vS", c)
                i = v.tensor_tensor(out=wf32[0:64, 1024:1536],
                                    in0=wf32[0:64, 1024:1536],
                                    in1=wf32[0:64, 0:512], op=ALU.subtract)
                c = sems.inc(i, "vS"); sems.w(v, "vS", c)
                i = v.tensor_tensor(out=wf32[0:64, 1024:1536],
                                    in0=wf32[0:64, 1024:1536],
                                    in1=wf32[0:64, 1536:2048], op=ALU.mult)
                c = sems.inc(i, "vS"); sems.w(v, "vS", c)
                i = v.tensor_tensor(out=acc32[0:64, tc * 512:(tc + 1) * 512],
                                    in0=wf32[0:64, 1024:1536],
                                    in1=wf32[0:64, 0:512], op=ALU.add)
                sems.inc(i, "vS")
            VE(spv2)

        if cut <= 2:
            return
        for ci in range(18):
            wsrc = ("wxs", ci) if ci < 16 else ("wbc", ci - 16)

            def ldwx(g, wsrc=wsrc, ci=ci):
                sems.w(g, "tS")
                wn, wi = wsrc
                g.dma_start(
                    out=stg2[:, :].rearrange("p (a d) -> p a d", a=16),
                    in_=E[wn][:, wi * 128:(wi + 1) * 128].rearrange(
                        "(a p) d -> p a d", p=128)
                ).then_inc(sems.h["dG"], 16)
                sems.c["dG"] += 16
            GP(ldwx)

            def ldcw(q, ci=ci):
                sems.w(q, "vS"); sems.w(q, "sS")
                dma(q, smb[:, 8:12], E["cwx"][ci * 128:(ci + 1) * 128, :])
                dma(q, smb[:, 12:13], E["cbx"][ci * 128:(ci + 1) * 128, :])
            SY(ldcw)

            for tc in range(4):
                def mmx(t, tc=tc, ci=ci):
                    sems.w(t, "dG")
                    sems.w(t, "vS")
                    bank = pb[tc % 2]
                    for cg in range(16):
                        ii = t.matmul(bank[:, :],
                                      stg2[:, cg * 128:(cg + 1) * 128],
                                      big0[:, cg * 2048 + tc * 512:
                                           cg * 2048 + (tc + 1) * 512],
                                      start=(cg == 0), stop=(cg == 15))
                    sems.inc(ii, "tS")
                TE(mmx)

                def evx(v, tc=tc):
                    sems.w(v, "tS")
                    iv = v.tensor_copy(out=xt32[:, tc * 512:(tc + 1) * 512],
                                       in_=pb[tc % 2][:, :])
                    sems.inc(iv, "vS")
                VE(evx)

            def cnv(v, ci=ci):
                sems.w(v, "dS"); sems.w(v, "vS")
                i = v.tensor_scalar(out=st32[:, 0:2048], in0=xt32[:, 0:2048],
                                    scalar1=smb[:, 11:12], scalar2=None,
                                    op0=ALU.mult)
                c = sems.inc(i, "vS"); sems.w(v, "vS", c)
                i = v.scalar_tensor_tensor(out=st32[:, 1:2048],
                                           in0=xt32[:, 0:2047],
                                           scalar=smb[:, 10:11],
                                           in1=st32[:, 1:2048],
                                           op0=ALU.mult, op1=ALU.add)
                c = sems.inc(i, "vS"); sems.w(v, "vS", c)
                i = v.scalar_tensor_tensor(out=st32[:, 2:2048],
                                           in0=xt32[:, 0:2046],
                                           scalar=smb[:, 9:10],
                                           in1=st32[:, 2:2048],
                                           op0=ALU.mult, op1=ALU.add)
                c = sems.inc(i, "vS"); sems.w(v, "vS", c)
                i = v.scalar_tensor_tensor(out=st32[:, 3:2048],
                                           in0=xt32[:, 0:2045],
                                           scalar=smb[:, 8:9],
                                           in1=st32[:, 3:2048],
                                           op0=ALU.mult, op1=ALU.add)
                sems.inc(i, "vS")
            VE(cnv)

            dstap = (stg[:, :] if ci < 16 else
                     big2[:, (ci - 16) * 2048:(ci - 15) * 2048])

            def slu(s, dstap=dstap):
                sems.w(s, "vS")
                ii = s.activation(out=dstap, in_=st32[:, 0:2048],
                                  func=AF.Silu, bias=smb[:, 12:13])
                sems.inc(ii, "sS")
            SC(slu)
            if ci < 16:
                transposes([stg[:, tt * 128:(tt + 1) * 128]
                            for tt in range(NT)],
                           [big1[:, tt * 2048 + ci * 128:
                                 tt * 2048 + (ci + 1) * 128]
                            for tt in range(NT)],
                           wait_first=lambda t: sems.w(t, "sS"))
            elif ci == 16:
                def cpb(v):
                    sems.w(v, "sS")
                    iv = v.tensor_copy(out=stg[:, :], in_=big2[:, 0:2048])
                    sems.inc(iv, "vS")
                VE(cpb)
                transposes([stg[:, tt * 128:(tt + 1) * 128]
                            for tt in range(NT)],
                           [big2[:, 4096 + tt * 128:4096 + (tt + 1) * 128]
                            for tt in range(NT)],
                           wait_first=lambda t: sems.w(t, "vS"))

        if cut <= 3:
            return
        if debug:
            def dbg1(q):
                sems.w(q, "vS"); sems.w(q, "sS")
                for tt in range(NT):
                    dma(q, E["dbg_xssm"][tt * 128:(tt + 1) * 128, :],
                        big1[:, tt * 2048:(tt + 1) * 2048])
                dma(q, E["dbg_bc"][0:128, :], big2[:, 0:2048])
                dma(q, E["dbg_bc"][128:256, :], big2[:, 2048:4096])
                dma(q, E["dbg_dt"][0:64, :], acc32[0:64, 0:2048])
            SY(dbg1)

            def dbg1w(v):
                sems.w(v, "dS")
                vnop(v)
            VE(dbg1w)
        def prep1(q):
            dma(q, smb[0:32, 16:17], E["alog"][:, :])
            dma(q, srow[0:1, 512:544], E["dvec"][:, :])
        SY(prep1)

        def prep2(s):
            sems.w(s, "dS")
            i = s.activation(out=smb[0:32, 17:18], in_=smb[0:32, 16:17],
                             func=AF.Exp)
            sems.inc(i, "sS")
        SC(prep2)

        def prep3(v):
            sems.w(v, "sS")
            i = v.tensor_scalar_mul(smb[0:32, 17:18], smb[0:32, 17:18], -1.0)
            c = sems.inc(i, "vS"); sems.w(v, "vS", c)
            i = v.tensor_scalar(out=acc32[0:32, 2048:4096],
                                in0=acc32[0:32, 0:2048],
                                scalar1=smb[0:32, 17:18], scalar2=None,
                                op0=ALU.mult)
            c = sems.inc(i, "vS"); sems.w(v, "vS", c)
            for ch in range(NCH):
                i = v.tensor_tensor_scan(
                    out=xt32[0:32, ch * 128:(ch + 1) * 128],
                    data0=onesf[0:32, :],
                    data1=acc32[0:32, 2048 + ch * 128:2048 + (ch + 1) * 128],
                    initial=0.0, op0=ALU.mult, op1=ALU.add)
            c = sems.inc(i, "vS"); sems.w(v, "vS", c)
            vnop(v)
        VE(prep3)

        def prep4(s):
            sems.w(s, "vS")
            i = s.activation(out=st32[0:32, 0:2048], in_=xt32[0:32, 0:2048],
                             func=AF.Exp)
            sems.inc(i, "sS")
        SC(prep4)
        tsrc = ([xt32[0:32, ch * 128:(ch + 1) * 128] for ch in range(NCH)]
                + [st32[0:32, ch * 128:(ch + 1) * 128] for ch in range(NCH)]
                + [acc32[0:32, ch * 128:(ch + 1) * 128] for ch in range(NCH)])
        tdst = ([wf32[:, ch * 32:(ch + 1) * 32] for ch in range(NCH)]
                + [wf32[:, 1024 + ch * 32:1024 + (ch + 1) * 32] for ch in range(NCH)]
                + [wf32[:, 1536 + ch * 32:1536 + (ch + 1) * 32] for ch in range(NCH)])
        tsrc[32:48] = [acc32[0:32, ch * 128:(ch + 1) * 128] for ch in range(NCH)]
        for i0 in range(0, 48, 4):
            def tseg(t, i0=i0):
                if i0 == 0:
                    sems.w(t, "sS")
                else:
                    sems.w(t, "vS")
                for j in range(4):
                    ii = t.transpose(pb[5][:, j * 128:j * 128 + 32],
                                     tsrc[i0 + j], identf[0:32, 0:32])
                sems.inc(ii, "tS")
            TE(tseg)

            def vseg(v, i0=i0):
                sems.w(v, "tS")
                for j in range(4):
                    iv = v.tensor_copy(out=tdst[i0 + j],
                                       in_=pb[5][:, j * 128:j * 128 + 32])
                sems.inc(iv, "vS")
            VE(vseg)

        def prep5(v):
            i = v.tensor_scalar_mul(wf32[:, 512:1024], wf32[:, 0:512], -1.0)
            c = sems.inc(i, "vS"); sems.w(v, "vS", c)
            i = v.tensor_copy(out=smb[0:32, 18:34], in_=xt32[0:32, 127:2048:128])
            sems.inc(i, "vS")
        VE(prep5)

        def prep6(s):
            sems.w(s, "vS")
            i = s.activation(out=smb[0:32, 18:34], in_=smb[0:32, 18:34],
                             func=AF.Exp)
            sems.inc(i, "sS")
        SC(prep6)

        def ldde(q):
            dma(q, st32[0:32, 1024:1536], E["deltam"][:, :])
        SY(ldde)

        def prep7(v):
            sems.w(v, "sS"); sems.w(v, "dS")
            i = v.tensor_tensor(
                out=st32[0:32, 1024:1536].rearrange("p (a b) -> p a b", a=16),
                in0=st32[0:32, 1024:1536].rearrange("p (a b) -> p a b", a=16),
                in1=smb[0:32, 18:34].to_broadcast([32, 16, 32]),
                op=ALU.mult)
            c = sems.inc(i, "vS"); sems.w(v, "vS", c)
            v.tensor_copy(out=stg[0:32, 0:512], in_=st32[0:32, 1024:1536])
            i = v.memset(stg[0:32, 512:640], 1.0)
            c = sems.inc(i, "vS"); sems.w(v, "vS", c)
            i = v.tensor_copy(out=srowb[0:1, 0:32], in_=srow[0:1, 512:544])
            sems.inc(i, "vS")
        VE(prep7)

        def prep8(t):
            sems.w(t, "vS")
            t.matmul(pb[5][:, :], stg[0:32, 512:640], stg[0:32, 0:512],
                     start=True, stop=True)
            ii = t.matmul(pb[6][:, 0:32], onesb[0:1, :], srowb[0:1, 0:32],
                          start=True, stop=True)
            sems.inc(ii, "tS")
        TE(prep8)

        def prep9(v):
            sems.w(v, "tS")
            v.tensor_copy(out=st32[:, 0:512], in_=pb[5][:, :])
            i = v.tensor_copy(out=st32[:, 512:544], in_=pb[6][:, 0:32])
            sems.inc(i, "vS")
        VE(prep9)

        for ch in range(NCH):
            def mmbc(t, ch=ch):
                if ch == 0:
                    sems.w(t, "vS")
                else:
                    sems.w(t, "vS")
                ii = t.matmul(pb[5][:, 0:128],
                              big2[:, ch * 128:(ch + 1) * 128],
                              big2[:, 2048 + ch * 128:2048 + (ch + 1) * 128],
                              start=True, stop=True)
                sems.inc(ii, "tS")
            TE(mmbc)

            def evbc(v, ch=ch):
                sems.w(v, "tS")
                i = v.tensor_tensor(out=big2[:, 6144 + ch * 128:
                                             6144 + (ch + 1) * 128],
                                    in0=pb[5][:, 0:128], in1=triuf[:, :],
                                    op=ALU.mult)
                sems.inc(i, "vS")
            VE(evbc)

        if cut <= 4:
            return
        for h in range(32):
            def hinit(g, h=h):
                sems.w(g, "tS")
                g.tensor_scalar(out=stg[:, 384:512], in0=identb[:, :],
                                scalar1=st32[:, 512 + h:513 + h], scalar2=None,
                                op0=ALU.mult)
                i = g.memset(stg[:, 512:576], 0.0)
                sems.inc(i, "gS")
            GP(hinit)
            for ch in range(NCH):
                cidx = ch * 32 + h

                def ebld(g, cidx=cidx):
                    sems.w(g, "tS")
                    i = g.tensor_scalar(out=xt32[:, 0:128], in0=onesf[:, :],
                                        scalar1=wf32[:, cidx:cidx + 1],
                                        scalar2=None, op0=ALU.mult)
                    sems.inc(i, "gS")
                GP(ebld)

                def etr(t):
                    sems.w(t, "gS")
                    ii = t.transpose(pb[3][:, 0:128], xt32[:, 0:128],
                                     identf[:, :])
                    sems.inc(ii, "tS")
                TE(etr)

                def ltv(v, cidx=cidx):
                    sems.w(v, "tS")
                    i = v.tensor_scalar(out=xt32[:, 128:256],
                                        in0=pb[3][:, 0:128],
                                        scalar1=wf32[:, 512 + cidx:513 + cidx],
                                        scalar2=0.0, op0=ALU.add, op1=ALU.min)
                    sems.inc(i, "vS")
                VE(ltv)

                def lt_(s):
                    sems.w(s, "vS")
                    ii = s.activation(out=xt32[:, 128:256],
                                      in_=xt32[:, 128:256], func=AF.Exp)
                    sems.inc(ii, "sS")
                SC(lt_)

                def mts(v, cidx=cidx, ch=ch):
                    sems.w(v, "sS")
                    i = v.scalar_tensor_tensor(
                        out=stg[:, 0:128], in0=xt32[:, 128:256],
                        scalar=wf32[:, 1536 + cidx:1537 + cidx],
                        in1=big2[:, 6144 + ch * 128:6144 + (ch + 1) * 128],
                        op0=ALU.mult, op1=ALU.mult)
                    i = v.tensor_scalar(out=xt32[:, 256:257],
                                        in0=xt32[:, 255:256],
                                        scalar1=wf32[:, 1536 + cidx:1537 + cidx],
                                        scalar2=None, op0=ALU.mult)
                    c = sems.inc(i, "vS"); sems.w(v, "vS", c)
                    i = v.tensor_scalar(out=stg[:, 128:256],
                                        in0=big2[:, 4096 + ch * 128:
                                                 4096 + (ch + 1) * 128],
                                        scalar1=xt32[:, 256:257], scalar2=None,
                                        op0=ALU.mult)
                    i = v.tensor_scalar(out=stg[:, 256:384], in0=identb[:, :],
                                        scalar1=st32[:, cidx:cidx + 1],
                                        scalar2=None, op0=ALU.mult)
                    sems.inc(i, "vS")
                VE(mts)

                def mmg(t, ch=ch, h=h):
                    sems.w(t, "vS"); sems.w(t, "gS")
                    xs = big1[:, ch * 2048 + h * 64:ch * 2048 + h * 64 + 64]
                    t.matmul(pb[0][:, 0:64], stg[:, 0:128], xs,
                             start=True, stop=False)
                    t.matmul(pb[0][:, 0:64], stg[:, 384:512], xs,
                             start=False, stop=True)
                    t.matmul(pb[1][:, 0:64],
                             big2[:, 2048 + ch * 128:2048 + (ch + 1) * 128],
                             stg[:, 512:576], start=True, stop=True)
                    t.matmul(pb[2][:, 0:64], stg[:, 256:384], stg[:, 512:576],
                             start=True, stop=False)
                    ii = t.matmul(pb[2][:, 0:64], stg[:, 128:256], xs,
                                  start=False, stop=True)
                    sems.inc(ii, "tS")
                TE(mmg)

                def evy(v, ch=ch, h=h, cidx=cidx):
                    sems.w(v, "tS")
                    v.tensor_copy(out=stg[:, 512:576], in_=pb[2][:, 0:64])
                    i = v.tensor_copy(out=xt32[:, 512:576], in_=pb[0][:, 0:64])
                    c = sems.inc(i, "vS"); sems.w(v, "vS", c)
                    iv = v.scalar_tensor_tensor(
                        out=big0[:, ch * 2048 + h * 64:ch * 2048 + h * 64 + 64],
                        in0=pb[1][:, 0:64],
                        scalar=wf32[:, 1024 + cidx:1025 + cidx],
                        in1=xt32[:, 512:576], op0=ALU.mult, op1=ALU.add)
                    sems.inc(iv, "vS")
                VE(evy)

        if cut <= 5:
            return
        if debug:
            def dbg2(q):
                sems.w(q, "vS")
                for tt in range(NT):
                    dma(q, E["dbg_y"][tt * 128:(tt + 1) * 128, :],
                        big0[:, tt * 2048:(tt + 1) * 2048])
            SY(dbg2)

            def dbg2w(v):
                sems.w(v, "dS")
                vnop(v)
            VE(dbg2w)
        def nwb1(q):
            sems.w(q, "vS")
            dma(q, xt32[0:1, 0:2048], E["normw"][0:1, :])
        SY(nwb1)

        def nwb1b(v):
            sems.w(v, "dS")
            i = v.tensor_copy(out=stg2[0:1, :], in_=xt32[0:1, :])
            sems.inc(i, "vS")
        VE(nwb1b)

        def nwb2(t):
            sems.w(t, "vS")
            t.matmul(pb[5][:, :], onesb[0:1, :], stg2[0:1, 0:512],
                     start=True, stop=True)
            ii = t.matmul(pb[6][:, :], onesb[0:1, :], stg2[0:1, 512:1024],
                          start=True, stop=True)
            sems.inc(ii, "tS")
        TE(nwb2)

        def nwb3(v):
            sems.w(v, "tS")
            v.tensor_copy(out=wf32[:, 0:512], in_=pb[5][:, :])
            i = v.tensor_copy(out=wf32[:, 512:1024], in_=pb[6][:, :])
            c = sems.inc(i, "vS"); sems.w(v, "vS", c)
            vnop(v)
        VE(nwb3)

        def nwb4(t):
            sems.w(t, "vS")
            t.matmul(pb[5][:, :], onesb[0:1, :], stg2[0:1, 1024:1536],
                     start=True, stop=True)
            ii = t.matmul(pb[6][:, :], onesb[0:1, :], stg2[0:1, 1536:2048],
                          start=True, stop=True)
            sems.inc(ii, "tS")
        TE(nwb4)

        def nwb5(v):
            sems.w(v, "tS")
            v.tensor_copy(out=wf32[:, 1024:1536], in_=pb[5][:, :])
            i = v.tensor_copy(out=wf32[:, 1536:2048], in_=pb[6][:, :])
            sems.inc(i, "vS")
        VE(nwb5)

        for tt in range(NT):
            def ldg(q, tt=tt):
                sems.w(q, "vS")
                dma(q, stg2[:, :], E["gate_sp"][tt * 128:(tt + 1) * 128, :])
            SY(ldg)

            def slg(s):
                sems.w(s, "dS")
                ii = s.activation(out=stg2[:, :], in_=stg2[:, :], func=AF.Silu)
                sems.inc(ii, "sS")
            SC(slg)

            def hsv(v, tt=tt):
                sems.w(v, "sS")
                i = v.tensor_tensor(out=acc32[:, 0:2048],
                                    in0=big0[:, tt * 2048:(tt + 1) * 2048],
                                    in1=stg2[:, :], op=ALU.mult)
                c = sems.inc(i, "vS"); sems.w(v, "vS", c)
                i = v.scalar_tensor_tensor(out=acc32[:, 2048:4096],
                                           in0=acc32[:, 0:2048], scalar=1.0,
                                           in1=acc32[:, 0:2048],
                                           op0=ALU.mult, op1=ALU.mult,
                                           accum_out=smb[:, 32 + tt:33 + tt])
                c = sems.inc(i, "vS"); sems.w(v, "vS", c)
                i = v.tensor_tensor(out=acc32[:, 0:2048], in0=acc32[:, 0:2048],
                                    in1=wf32[:, 0:2048], op=ALU.mult)
                c = sems.inc(i, "vS"); sems.w(v, "vS", c)
                i = v.tensor_copy(out=big0[:, tt * 2048:(tt + 1) * 2048],
                                  in_=acc32[:, 0:2048])
                sems.inc(i, "vS")
            VE(hsv)


        for tt in range(NT):
            transposes([big0[:, tt * 2048 + c * 128:tt * 2048 + (c + 1) * 128]
                        for c in range(NC)],
                       [big1[:, c * 2048 + tt * 128:c * 2048 + (tt + 1) * 128]
                        for c in range(NC)],
                       wait_first=lambda t: sems.w(t, "vS"))

        if cut <= 6:
            return
        for dc in range(4):
            def ldwo(g, dc=dc):
                sems.w(g, "tS")
                g.dma_start(
                    out=big2[:, :].rearrange("p (a d) -> p a d", a=16),
                    in_=E["wout"][:, dc * 512:(dc + 1) * 512].rearrange(
                        "(a p) d -> p a d", p=128)
                ).then_inc(sems.h["dG"], 16)
                sems.c["dG"] += 16
            GP(ldwo)
            for tt in range(NT):
                def mmo(t, tt=tt, dc=dc):
                    sems.w(t, "dG"); sems.w(t, "vS")
                    bank = pb[tt % 2]
                    for cg in range(16):
                        ii = t.matmul(bank[:, :],
                                      big1[:, cg * 2048 + tt * 128:
                                           cg * 2048 + (tt + 1) * 128],
                                      big2[:, cg * 512:(cg + 1) * 512],
                                      start=(cg == 0), stop=(cg == 15))
                    sems.inc(ii, "tS")
                TE(mmo)

                def evo(v, tt=tt, dc=dc):
                    sems.w(v, "tS")
                    i = v.tensor_copy(
                        out=big0[:, tt * 2048 + dc * 512:
                                 tt * 2048 + (dc + 1) * 512],
                        in_=pb[tt % 2][:, :])
                    sems.inc(i, "vS")
                VE(evo)
        def ssb(v):
            sems.w(v, "vS")
            i = v.tensor_copy(out=stg[:, 0:16], in_=smb[:, 32:48])
            i = v.memset(stg[:, 16:144], 0.0)
            sems.inc(i, "vS")
        VE(ssb)
        for tt in range(NT):
            def stpo(q, tt=tt):
                sems.w(q, "vS")
                dma(q, E["cc2_in"][tt * 128:(tt + 1) * 128, 0:D],
                    big0[:, tt * 2048:(tt + 1) * 2048])
                dma(q, E["cc2_in"][tt * 128:(tt + 1) * 128, D:D + 1],
                    stg[:, tt:tt + 1], slow=True)
                dma(q, E["cc2_in"][tt * 128:(tt + 1) * 128, D + 1:D + 128],
                    stg[:, 17:144])
            SY(stpo)

        def cc2(g):
            sems.w(g, "dS")
            ii = g.collective_compute(
                "ReduceScatter", ALU.add, replica_groups=RG_PAIRS,
                ins=[E["cc2_in"][:, :]], outs=[E["cc2_out"][:, :]])
            sems.inc(ii, "ccS")
        GP(cc2)
        if debug:
            def dbg3(q):
                sems.w(q, "ccS")
                for tt in range(NTO):
                    dma(q, E["dbg_cc2"][tt * 128:(tt + 1) * 128, :],
                        E["cc2_out"][tt * 128:(tt + 1) * 128, :])
            SY(dbg3)

        if cut <= 7:
            return
        def ldwr(q):
            sems.w(q, "ccS")
            dma(q, xt32[0:1, 0:2048], E["wrouter"][0:1, :])
        SY(ldwr)

        def ldwrb(v):
            sems.w(v, "dS")
            i = v.tensor_copy(out=stg2[0:1, :], in_=xt32[0:1, :])
            sems.inc(i, "vS")
        VE(ldwrb)

        def wrb1(t):
            sems.w(t, "vS")
            for j in range(4):
                ii = t.matmul(pb[2 + j][:, :], onesb[0:1, :],
                              stg2[0:1, j * 512:(j + 1) * 512],
                              start=True, stop=True)
            sems.inc(ii, "tS")
        TE(wrb1)

        def wrb2(v):
            sems.w(v, "tS")
            for j in range(4):
                i = v.tensor_copy(out=wf32[:, j * 512:(j + 1) * 512],
                                  in_=pb[2 + j][:, :])
            sems.inc(i, "vS")
        VE(wrb2)
        for tt in range(NTO):
            def ldmo(q, tt=tt):
                sems.w(q, "vS")
                dma(q, stg2[:, :], E["cc2_out"][tt * 128:(tt + 1) * 128, 0:D])
                dma(q, stg[:, 100:101],
                    E["cc2_out"][tt * 128:(tt + 1) * 128, D:D + 1], slow=True)
                dma(q, xt32[:, :], E["x2_sp"][tt * 128:(tt + 1) * 128, :])
            SY(ldmo)

            def rsq(v):
                sems.w(v, "dS")
                i = v.tensor_scalar(out=smb[:, 1:2], in0=stg[:, 100:101],
                                    scalar1=1.0 / INTER, scalar2=1e-5,
                                    op0=ALU.mult, op1=ALU.add)
                sems.inc(i, "vS")
            VE(rsq)

            def rsqs(s):
                sems.w(s, "vS")
                i = s.activation(out=smb[:, 1:2], in_=smb[:, 1:2], func=AF.Ln)
                c = sems.inc(i, "sS"); sems.w(s, "sS", c)
                i = s.activation(out=smb[:, 1:2], in_=smb[:, 1:2], func=AF.Exp,
                                 scale=-0.5)
                sems.inc(i, "sS")
            SC(rsqs)

            def adm(v):
                sems.w(v, "sS")
                i = v.scalar_tensor_tensor(out=st32[:, :], in0=stg2[:, :],
                                           scalar=smb[:, 1:2], in1=xt32[:, :],
                                           op0=ALU.mult, op1=ALU.add)
                sems.inc(i, "vS")
            VE(adm)
            ln(st32[:, :], D, 1e-5, st32[:, :], scratch=acc32[:, 0:2048],
               out_bf=stg[:, :])

            def lgt(v, tt=tt):
                i = v.tensor_tensor(out=acc32[:, 0:2048], in0=st32[:, :],
                                    in1=wf32[:, 0:2048], op=ALU.mult)
                c = sems.inc(i, "vS"); sems.w(v, "vS", c)
                i = v.reduce_sum(out=smb[:, 2:3], in_=acc32[:, 0:2048],
                                 axis=AX.X)
                c = sems.inc(i, "vS"); sems.w(v, "vS", c)
                i = v.tensor_copy(out=big2[:, tt:tt + 1], in_=smb[:, 2:3])
                c = sems.inc(i, "vS"); sems.w(v, "vS", c)
                i = v.tensor_copy(out=smb[:, 3:4], in_=big2[:, tt:tt + 1])
                c = sems.inc(i, "vS"); sems.w(v, "vS", c)
                i = v.tensor_tensor(out=smb[:, 4:5], in0=smb[:, 2:3],
                                    in1=smb[:, 3:4], op=ALU.subtract)
                c = sems.inc(i, "vS"); sems.w(v, "vS", c)
                i = v.tensor_copy(out=big2[:, 16 + tt:17 + tt],
                                  in_=smb[:, 4:5])
                sems.inc(i, "vS")
            VE(lgt)

            def stx3(q, tt=tt):
                sems.w(q, "vS")
                dma(q, E["x3_sp"][tt * 128:(tt + 1) * 128, :], st32[:, :])
                dma(q, E["cc3_in"][tt * 128:(tt + 1) * 128, :], stg[:, :])
            SY(stx3)

            def wst3(v):
                sems.w(v, "dS")
                vnop(v)
            VE(wst3)
        def lgtr(t):
            sems.w(t, "vS")
            t.transpose(ptb[0:8, 0:128], big2[:, 0:8], identb[:, :])
            ii = t.transpose(ptb[0:8, 128:256], big2[:, 16:24], identb[:, :])
            sems.inc(ii, "tS")
        TE(lgtr)

        def lgtr2(v):
            sems.w(v, "tS")
            v.tensor_copy(out=stg[0:8, 0:128], in_=ptb[0:8, 0:128])
            i = v.tensor_copy(out=stg[0:8, 128:256], in_=ptb[0:8, 128:256])
            sems.inc(i, "vS")
        VE(lgtr2)

        def stlg(q):
            sems.w(q, "vS")
            for b in range(8):
                dma(q, E["cc3_in"][HALF:HALF + 1, b * 128:(b + 1) * 128],
                    stg[b:b + 1, 0:128])
                dma(q, E["cc3_in"][HALF + 1:HALF + 2, b * 128:(b + 1) * 128],
                    stg[b:b + 1, 128:256])
        SY(stlg)

        def cc3(g):
            sems.w(g, "dS")
            ii = g.collective_compute(
                "AllGather", ALU.bypass, replica_groups=RG_PAIRS,
                ins=[E["cc3_in"][:, :]], outs=[E["cc3_out"][:, :]])
            sems.inc(ii, "ccS")
        GP(cc3)


    def p_block4():
        R1H = HALF + 2 + HALF
        def r1(q):
            sems.w(q, "ccS"); sems.w(q, "vS"); sems.w(q, "tS")
            dma(q, stg[0:1, 0:1024], E["cc3_out"][HALF:HALF + 1, 0:1024])
            dma(q, stg[0:1, 1024:2048], E["cc3_out"][R1H:R1H + 1, 0:1024])
            dma(q, stg2[0:1, 0:1024],
                E["cc3_out"][HALF + 1:HALF + 2, 0:1024])
            dma(q, stg2[0:1, 1024:2048],
                E["cc3_out"][R1H + 1:R1H + 2, 0:1024])
            for rr, c0 in ((HALF, 0), (R1H, 8), (HALF + 1, 16), (R1H + 1, 24)):
                q.dma_start(out=big2[:, 100 + c0:108 + c0],
                            in_=E["cc3_out"][rr:rr + 1, 0:1024].rearrange(
                                "r (a p) -> p (r a)", p=128),
                            allow_slow_non_contiguous=True
                            ).then_inc(sems.h["dS"], 16)
                sems.c["dS"] += 16
            dma(q, acc32[:, 0:4096], E["bigmask"][:, :])
        SY(r1)

        def r2(v):
            sems.w(v, "dS")
            v.tensor_copy(out=st32[:, 0:16], in_=big2[:, 100:116])
            i = v.tensor_copy(out=st32[:, 16:32], in_=big2[:, 116:132])
            c = sems.inc(i, "vS"); sems.w(v, "vS", c)
            i = v.tensor_tensor(out=st32[:, 0:16], in0=st32[:, 0:16],
                                in1=st32[:, 16:32], op=ALU.add)
            sems.inc(i, "vS")
        VE(r2)

        def r3(t):
            sems.w(t, "vS")
            for j in range(4):
                t.matmul(pb[j][:, :], onesb[0:1, :],
                         stg[0:1, j * 512:(j + 1) * 512],
                         start=True, stop=True)
            for j in range(2):
                ii = t.matmul(pb[4 + j][:, :], onesb[0:1, :],
                              stg2[0:1, j * 512:(j + 1) * 512],
                              start=True, stop=True)
            sems.inc(ii, "tS")
        TE(r3)

        def r4(v):
            sems.w(v, "tS")
            for j in range(4):
                i = v.tensor_copy(out=wf32[:, j * 512:(j + 1) * 512],
                                  in_=pb[j][:, :])
            c = sems.inc(i, "vS"); sems.w(v, "vS", c)
            for j in range(2):
                i = v.tensor_tensor(out=wf32[:, j * 512:(j + 1) * 512],
                                    in0=wf32[:, j * 512:(j + 1) * 512],
                                    in1=pb[4 + j][:, :], op=ALU.add)
            sems.inc(i, "vS")
        VE(r4)

        def r3b(t):
            sems.w(t, "vS")
            for j in range(2):
                ii = t.matmul(pb[4 + j][:, :], onesb[0:1, :],
                              stg2[0:1, 1024 + j * 512:1024 + (j + 1) * 512],
                              start=True, stop=True)
            sems.inc(ii, "tS")
        TE(r3b)

        def r4b(v):
            sems.w(v, "tS")
            for j in range(2):
                i = v.tensor_tensor(
                    out=wf32[:, 1024 + j * 512:1024 + (j + 1) * 512],
                    in0=wf32[:, 1024 + j * 512:1024 + (j + 1) * 512],
                    in1=pb[4 + j][:, :], op=ALU.add)
            sems.inc(i, "vS")
        VE(r4b)
        for it in range(16):
            def rk(v, it=it):
                v.tensor_scalar(out=xt32[:, 0:2048], in0=wf32[:, 0:2048],
                                scalar1=st32[:, it:it + 1], scalar2=0.0,
                                op0=ALU.is_gt, op1=ALU.add,
                                accum_out=sm[:, it:it + 1])
                i = v.scalar_tensor_tensor(
                    out=xt32[:, 0:2048], in0=wf32[:, 0:2048],
                    scalar=st32[:, it:it + 1],
                    in1=acc32[:, 2048 - 128 * it:4096 - 128 * it],
                    op0=ALU.is_equal, op1=ALU.mult,
                    accum_out=smb[:, it:it + 1])
                sems.inc(i, "vS")
            VE(rk)

        def msel(v):
            sems.w(v, "vS")
            i = v.tensor_tensor(out=sm[:, 16:32], in0=sm[:, 0:16],
                                in1=smb[:, 0:16], op=ALU.add)
            c = sems.inc(i, "vS"); sems.w(v, "vS", c)
            i = v.tensor_scalar(out=smb[:, 16:32], in0=sm[:, 16:32],
                                scalar1=float(TOPK), scalar2=None,
                                op0=ALU.is_lt)
            c = sems.inc(i, "vS"); sems.w(v, "vS", c)
            v.tensor_copy(out=stg[:, 0:16], in_=smb[:, 16:32])
            i = v.memset(stg2[:, 0:128], 1.0)
            sems.inc(i, "vS")
        VE(msel)
        for it in range(16):
            def pfx(t, it=it):
                sems.w(t, "vS")
                first = True
                for jt in range(it):
                    t.matmul(pb[4][:, it:it + 1], stg2[:, 0:128],
                             stg[:, jt:jt + 1], start=first, stop=False)
                    first = False
                ii = t.matmul(pb[4][:, it:it + 1], trisb[:, :],
                              stg[:, it:it + 1], start=first, stop=True)
                sems.inc(ii, "tS")
            TE(pfx)

        def psel(v):
            sems.w(v, "tS")
            i = v.tensor_copy(out=sm[:, 32:48], in_=pb[4][:, 0:16])
            c = sems.inc(i, "vS"); sems.w(v, "vS", c)
            i = v.scalar_tensor_tensor(out=sm[:, 32:48], in0=sm[:, 32:48],
                                       scalar=1.0, in1=smb[:, 16:32],
                                       op0=ALU.add, op1=ALU.mult)
            c = sems.inc(i, "vS"); sems.w(v, "vS", c)
            i = v.tensor_scalar_add(sm[:, 32:48], sm[:, 32:48], -1.0)
            c = sems.inc(i, "vS"); sems.w(v, "vS", c)
            vnop(v)
        VE(psel)

        def ldio(q):
            sems.w(q, "vS")
            dma(q, acc32[:, 0:512], E["iota_own"][:, :])
            dma(q, smb[:, 50:58], E["iota_posc"][:, :])
        SY(ldio)
        for it in range(16):
            def oht(v, it=it):
                sems.w(v, "dS")
                i = v.tensor_scalar(out=big2[:, it * 512:(it + 1) * 512],
                                    in0=acc32[:, 0:512],
                                    scalar1=sm[:, 32 + it:33 + it],
                                    scalar2=None, op0=ALU.is_equal)
                sems.inc(i, "vS")
            VE(oht)
        def prt(t):
            sems.w(t, "vS")
            ii = t.transpose(pb[4][0:16, 0:128], sm[:, 32:48], identf[:, :])
            sems.inc(ii, "tS")
        TE(prt)

        def prt2(v):
            sems.w(v, "tS")
            i = v.tensor_copy(out=st32[0:16, 32:160], in_=pb[4][0:16, 0:128])
            sems.inc(i, "vS")
        VE(prt2)

        def prt3(q):
            sems.w(q, "vS")
            dma(q, xt32[0:1, 0:2048], st32[0:16, 32:160])
        SY(prt3)

        def prt4(v):
            sems.w(v, "dS")
            i = v.tensor_copy(out=itile[0:1, 0:1024], in_=xt32[0:1, 0:1024])
            c = sems.inc(i, "vS"); sems.w(v, "vS", c)
            i = v.tensor_scalar(out=itile[0:1, 0:1024], in0=itile[0:1, 0:1024],
                                scalar1=5, scalar2=None,
                                op0=ALU.arith_shift_right)
            c = sems.inc(i, "vS"); sems.w(v, "vS", c)
            i = v.tensor_copy(out=st32[0:1, 0:1024], in_=itile[0:1, 0:1024])
            c = sems.inc(i, "vS"); sems.w(v, "vS", c)
            i = v.tensor_copy(out=itile[0:1, 0:1024], in_=xt32[0:1, 1024:2048])
            c = sems.inc(i, "vS"); sems.w(v, "vS", c)
            i = v.tensor_scalar(out=itile[0:1, 0:1024], in0=itile[0:1, 0:1024],
                                scalar1=5, scalar2=None,
                                op0=ALU.arith_shift_right)
            c = sems.inc(i, "vS"); sems.w(v, "vS", c)
            i = v.tensor_copy(out=st32[0:1, 1024:2048], in_=itile[0:1, 0:1024])
            c = sems.inc(i, "vS"); sems.w(v, "vS", c)
            i = v.scalar_tensor_tensor(out=st32[0:1, 0:2048],
                                       in0=st32[0:1, 0:2048], scalar=-32.0,
                                       in1=xt32[0:1, 0:2048],
                                       op0=ALU.mult, op1=ALU.add)
            c = sems.inc(i, "vS"); sems.w(v, "vS", c)
            i = v.tensor_tensor(out=xt32[0:1, 0:2048], in0=xt32[0:1, 0:2048],
                                in1=st32[0:1, 0:2048], op=ALU.subtract)
            c = sems.inc(i, "vS"); sems.w(v, "vS", c)
            i = v.tensor_scalar_mul(xt32[0:1, 0:2048], xt32[0:1, 0:2048],
                                    1.0 / 32.0)
            c = sems.inc(i, "vS"); sems.w(v, "vS", c)
            v.tensor_copy(out=stg[0:1, 0:2048], in_=xt32[0:1, 0:2048])
            i = v.tensor_copy(out=stg2[0:1, 0:2048], in_=st32[0:1, 0:2048])
            sems.inc(i, "vS")
        VE(prt4)

        def prb2(t):
            sems.w(t, "vS")
            for j in range(4):
                t.matmul(pb[j][:, :], onesb[0:1, :],
                         stg[0:1, j * 512:(j + 1) * 512], start=True, stop=True)
            ii = t.matmul(pb[4][:, :], onesb[0:1, :], stg2[0:1, 0:512],
                          start=True, stop=True)
            ii = t.matmul(pb[5][:, :], onesb[0:1, :], stg2[0:1, 512:1024],
                          start=True, stop=True)
            sems.inc(ii, "tS")
        TE(prb2)

        def prbe2(v):
            sems.w(v, "tS")
            for j in range(4):
                i = v.tensor_copy(out=wf32[:, j * 512:(j + 1) * 512],
                                  in_=pb[j][:, :])
            c = sems.inc(i, "vS"); sems.w(v, "vS", c)
            for j in range(2):
                i = v.scalar_tensor_tensor(
                    out=wf32[:, j * 512:(j + 1) * 512],
                    in0=wf32[:, j * 512:(j + 1) * 512], scalar=32.0,
                    in1=pb[4 + j][:, :], op0=ALU.mult, op1=ALU.add)
            sems.inc(i, "vS")
        VE(prbe2)

        def prb3(t):
            sems.w(t, "vS")
            t.matmul(pb[4][:, :], onesb[0:1, :], stg2[0:1, 1024:1536],
                     start=True, stop=True)
            ii = t.matmul(pb[5][:, :], onesb[0:1, :], stg2[0:1, 1536:2048],
                          start=True, stop=True)
            sems.inc(ii, "tS")
        TE(prb3)

        def prbe3(v):
            sems.w(v, "tS")
            for j in range(2):
                i = v.scalar_tensor_tensor(
                    out=wf32[:, 1024 + j * 512:1024 + (j + 1) * 512],
                    in0=wf32[:, 1024 + j * 512:1024 + (j + 1) * 512],
                    scalar=32.0, in1=pb[4 + j][:, :],
                    op0=ALU.mult, op1=ALU.add)
            sems.inc(i, "vS")
        VE(prbe3)
        for pm in range(4):
            def ohs(v, pm=pm):
                i = v.tensor_scalar(
                    out=big1[:, 16384 + pm * 2048:16384 + (pm + 1) * 2048],
                    in0=wf32[:, 0:2048],
                    scalar1=smb[:, 50 + pm:51 + pm],
                    scalar2=None, op0=ALU.is_equal)
                sems.inc(i, "vS")
            VE(ohs)
        def rw2(v):
            i = v.reduce_max(out=sm[:, 60:61], in_=st32[:, 0:16], axis=AX.X)
            sems.inc(i, "vS")
        VE(rw2)

        def rw3b(t):
            sems.w(t, "vS")
            ii = t.transpose(pb[4][0:1, 0:128], sm[:, 60:61], identf[:, :])
            sems.inc(ii, "tS")
        TE(rw3b)

        def rw4(v):
            sems.w(v, "tS")
            i = v.tensor_copy(out=st32[0:1, 16:144], in_=pb[4][0:1, 0:128])
            c = sems.inc(i, "vS"); sems.w(v, "vS", c)
            i = v.reduce_max(out=st32[0:1, 150:151], in_=st32[0:1, 16:144],
                             axis=AX.X, negate=True)
            c = sems.inc(i, "vS"); sems.w(v, "vS", c)
            i = v.tensor_copy(out=srowb[0:1, 0:1], in_=st32[0:1, 150:151])
            sems.inc(i, "vS")
        VE(rw4)

        def rw5(t):
            sems.w(t, "vS")
            ii = t.matmul(pb[4][:, 0:1], onesb[0:1, :], srowb[0:1, 0:1],
                          start=True, stop=True)
            sems.inc(ii, "tS")
        TE(rw5)

        def rw6(v):
            sems.w(v, "tS")
            i = v.tensor_copy(out=sm[:, 61:62], in_=pb[4][:, 0:1])
            sems.inc(i, "vS")
        VE(rw6)

        def rw7b(s):
            ii = s.activation(out=sm[:, 0:16], in_=st32[:, 0:16],
                              func=AF.Exp, bias=sm[:, 61:62])
            sems.inc(ii, "sS")
        SC(rw7b)

        def rw8(v):
            sems.w(v, "sS")
            i = v.tensor_tensor(out=sm[:, 0:16], in0=sm[:, 0:16],
                                in1=smb[:, 16:32], op=ALU.mult)
            c = sems.inc(i, "vS"); sems.w(v, "vS", c)
            i = v.tensor_copy(out=stg[:, 32:48], in_=sm[:, 0:16])
            sems.inc(i, "vS")
        VE(rw8)

        def rw9(t):
            sems.w(t, "vS")
            ii = t.matmul(pb[4][0:1, 0:16], onescol[:, 0:1], stg[:, 32:48],
                          start=True, stop=True)
            sems.inc(ii, "tS")
        TE(rw9)

        def rw10(v):
            sems.w(v, "tS")
            i = v.tensor_copy(out=st32[0:1, 160:176], in_=pb[4][0:1, 0:16])
            c = sems.inc(i, "vS"); sems.w(v, "vS", c)
            i = v.reduce_sum(out=st32[0:1, 176:177], in_=st32[0:1, 160:176],
                             axis=AX.X)
            c = sems.inc(i, "vS"); sems.w(v, "vS", c)
            i = v.reciprocal(out=st32[0:1, 177:178], in_=st32[0:1, 176:177])
            c = sems.inc(i, "vS"); sems.w(v, "vS", c)
            i = v.tensor_copy(out=srowb[0:1, 1:2], in_=st32[0:1, 177:178])
            sems.inc(i, "vS")
        VE(rw10)

        def rw11(t):
            sems.w(t, "vS")
            ii = t.matmul(pb[4][:, 0:1], onesb[0:1, :], srowb[0:1, 1:2],
                          start=True, stop=True)
            sems.inc(ii, "tS")
        TE(rw11)

        def rw12(v):
            sems.w(v, "tS")
            i = v.tensor_copy(out=sm[:, 62:63], in_=pb[4][:, 0:1])
            c = sems.inc(i, "vS"); sems.w(v, "vS", c)
            i = v.tensor_scalar(out=sm[:, 0:16], in0=sm[:, 0:16],
                                scalar1=sm[:, 62:63], scalar2=None,
                                op0=ALU.mult)
            c = sems.inc(i, "vS"); sems.w(v, "vS", c)
            i = v.tensor_copy(out=stg[:, 32:48], in_=sm[:, 0:16])
            sems.inc(i, "vS")
        VE(rw12)
        for ps in range(4):
            def rwp(t, ps=ps):
                sems.w(t, "vS")
                for it in range(16):
                    ii = t.matmul(pb[5][:, ps:ps + 1],
                                  big2[:, it * 512 + ps * 128:
                                       it * 512 + (ps + 1) * 128],
                                  stg[:, 32 + it:33 + it],
                                  start=(it == 0), stop=(it == 15))
                sems.inc(ii, "tS")
            TE(rwp)

        def rwpe(v):
            sems.w(v, "tS")
            i = v.tensor_copy(out=sm[:, 52:56], in_=pb[5][:, 0:4])
            sems.inc(i, "vS")
        VE(rwpe)

        def tb4(ps, c0=0, c1=D):
            tns, base = [(xt32, 0), (st32, 0), (acc32, 0), (acc32, 2048)][ps]
            return tns[:, base + c0:base + c1]
        for tt in range(NT):
            def ldx3(q, tt=tt):
                if tt == 0:
                    sems.w(q, "vS"); sems.w(q, "tS")
                r0 = tt * 128 if tt < 8 else HALF + 2 + (tt - 8) * 128
                dma(q, big0[:, tt * 2048:(tt + 1) * 2048],
                    E["cc3_out"][r0:r0 + 128, :])
            SY(ldx3)
        for ps in range(4):
            for dc2 in range(4):
                def mmf(t, ps=ps, dc2=dc2):
                    sems.w(t, "dS"); sems.w(t, "vS")
                    bank = pb[dc2 % 2]
                    for it in range(16):
                        ii = t.matmul(bank[:, :],
                                      big2[:, it * 512 + ps * 128:
                                           it * 512 + (ps + 1) * 128],
                                      big0[:, it * 2048 + dc2 * 512:
                                           it * 2048 + (dc2 + 1) * 512],
                                      start=(it == 0), stop=(it == 15))
                    sems.inc(ii, "tS")
                TE(mmf)

                def evf(v, ps=ps, dc2=dc2):
                    sems.w(v, "tS")
                    i = v.tensor_copy(out=tb4(ps, dc2 * 512, (dc2 + 1) * 512),
                                      in_=pb[dc2 % 2][:, :])
                    sems.inc(i, "vS")
                VE(evf)
        def g1b(q):
            sems.w(q, "vS")
            dma(q, wf32[0:1, 0:2048], E["g1"][0:1, :])
        SY(g1b)

        def g1c(v):
            sems.w(v, "dS")
            i = v.tensor_copy(out=stg2[0:1, 0:2048], in_=wf32[0:1, 0:2048])
            sems.inc(i, "vS")
        VE(g1c)

        def g1m(t):
            sems.w(t, "vS")
            for j in range(4):
                ii = t.matmul(pb[2 + j][:, :], onesb[0:1, :],
                              stg2[0:1, j * 512:(j + 1) * 512],
                              start=True, stop=True)
            sems.inc(ii, "tS")
        TE(g1m)

        def g1e(v):
            sems.w(v, "tS")
            for j in range(4):
                i = v.tensor_copy(out=wf32[:, j * 512:(j + 1) * 512],
                                  in_=pb[2 + j][:, :])
            sems.inc(i, "vS")
        VE(g1e)
        for ps in range(4):
            ln(tb4(ps), D, 1e-6, tb4(ps), sub_mean=False,
               scratch=stg2[:, 0:2048], g_ap=wf32[:, 0:2048],
               out_bf=stg[:, :])
            transposes([stg[:, c * 128:(c + 1) * 128] for c in range(NC)],
                       [big1[:, c * 512 + ps * 128:c * 512 + (ps + 1) * 128]
                        for c in range(NC)],
                       wait_first=lambda t: sems.w(t, "vS"))


        for ps in range(4):
            def ldcm(q, ps=ps):
                sems.w(q, "vS")
                dma(q, cosb[:, 0:256], E["cosM4"][ps * 128:(ps + 1) * 128, :])
                dma(q, sinb[:, 0:256], E["sinM4"][ps * 128:(ps + 1) * 128, :])
            SY(ldcm)

            def wcm(v):
                sems.w(v, "dS")
                vnop(v)
            VE(wcm)
            for wi, wn in enumerate(("wq2", "wk2", "wv2")):
                for dc2 in range(4):
                    def ldwc(g, wn=wn, dc2=dc2):
                        sems.w(g, "tS")
                        g.dma_start(
                            out=big2[:, :].rearrange("p (a d) -> p a d", a=16),
                            in_=E[wn][:, dc2 * 512:(dc2 + 1) * 512].rearrange(
                                "(a p) d -> p a d", p=128)
                        ).then_inc(sems.h["dG"], 16)
                        sems.c["dG"] += 16
                    GP(ldwc)

                    def mmq(t, ps=ps, dc2=dc2):
                        sems.w(t, "dG"); sems.w(t, "vS")
                        for cg in range(16):
                            ii = t.matmul(pb[0][:, :],
                                          big1[:, cg * 512 + ps * 128:
                                               cg * 512 + (ps + 1) * 128],
                                          big2[:, cg * 512:(cg + 1) * 512],
                                          start=(cg == 0), stop=(cg == 15))
                        sems.inc(ii, "tS")
                    TE(mmq)
                    if wn != "wv2":
                        def evq(v, ps=ps, dc2=dc2, wn=wn):
                            sems.w(v, "tS")
                            ev3 = pb[0][:, 0:512].rearrange(
                                "p (h t) -> p h t", h=4)
                            evh = ev3[:, :, 0:64]
                            odh = ev3[:, :, 64:128]
                            c3 = cosb[:, 0:256].rearrange(
                                "p (h f) -> p h f", h=4)
                            s3 = sinb[:, 0:256].rearrange(
                                "p (h f) -> p h f", h=4)
                            a32 = wf32[:, 0:1024].rearrange(
                                "p (x h f) -> p x h f", x=4, h=4)
                            v.tensor_tensor(out=a32[:, 0], in0=evh, in1=c3,
                                            op=ALU.mult)
                            v.tensor_tensor(out=a32[:, 1], in0=odh, in1=s3,
                                            op=ALU.mult)
                            v.tensor_tensor(out=a32[:, 2], in0=evh, in1=s3,
                                            op=ALU.mult)
                            i = v.tensor_tensor(out=a32[:, 3], in0=odh, in1=c3,
                                                op=ALU.mult)
                            c = sems.inc(i, "vS"); sems.w(v, "vS", c)
                            st3 = stg2[:, 0:512].rearrange(
                                "p (h t) -> p h t", h=4)
                            v.tensor_tensor(out=st3[:, :, 0:64], in0=a32[:, 0],
                                            in1=a32[:, 1], op=ALU.subtract)
                            i = v.tensor_tensor(out=st3[:, :, 64:128],
                                                in0=a32[:, 2], in1=a32[:, 3],
                                                op=ALU.add)
                            sems.inc(i, "vS")
                        VE(evq)
                    else:
                        def evq(v):
                            sems.w(v, "tS")
                            i = v.tensor_copy(out=stg2[:, 0:512],
                                              in_=pb[0][:, :])
                            sems.inc(i, "vS")
                        VE(evq)
                    if wn == "wq2":
                        def mvq(v, ps=ps, dc2=dc2):
                            i = v.tensor_copy(
                                out=big1[:, 8192 + ps * 2048 + dc2 * 512:
                                         8192 + ps * 2048 + (dc2 + 1) * 512],
                                in_=stg2[:, 0:512])
                            sems.inc(i, "vS")
                        VE(mvq)
                    else:
                        r0 = (0 if wn == "wk2" else 512) + ps * 128

                        def stkv(q, r0=r0, dc2=dc2):
                            sems.w(q, "vS")
                            dma(q, E["cc4_in"][r0:r0 + 128,
                                               dc2 * 512:(dc2 + 1) * 512],
                                stg2[:, 0:512])
                        SY(stkv)

                        def wkv(v):
                            sems.w(v, "dS")
                            vnop(v)
                        VE(wkv)

        def cc4(g):
            sems.w(g, "dS"); sems.w(g, "vS")
            ii = g.collective_compute(
                "AllGather", ALU.bypass, replica_groups=RG_PAIRS,
                ins=[E["cc4_in"][:, :]], outs=[E["cc4_out"][:, :]])
            sems.inc(ii, "ccS")
        GP(cc4)

        def wcc4(v):
            sems.w(v, "ccS")
            vnop(v)
        VE(wcc4)
        for ps in range(4):
            def mvq2(v, ps=ps):
                if ps == 0:
                    sems.w(v, "dS")
                i = v.tensor_copy(out=stg[:, :],
                                  in_=big1[:, 8192 + ps * 2048:
                                           8192 + (ps + 1) * 2048])
                sems.inc(i, "vS")
            VE(mvq2)
            transposes([stg[:, c * 128:(c + 1) * 128] for c in range(NC)],
                       [big2[:, c * 512 + ps * 128:c * 512 + (ps + 1) * 128]
                        for c in range(NC)],
                       wait_first=lambda t: sems.w(t, "vS"))

        kro = [0, 128, 256, 384, 1024, 1152, 1280, 1408]
        vro = [512, 640, 768, 896, 1536, 1664, 1792, 1920]
        for kt in range(8):
            def ldk2(q, kt=kt):
                sems.w(q, "vS"); sems.w(q, "tS")
                dma(q, stg[:, :], E["cc4_out"][kro[kt]:kro[kt] + 128, :])
            SY(ldk2)

            def wk2(v):
                sems.w(v, "dS")
                vnop(v)
            VE(wk2)
            transposes([stg[:, c * 128:(c + 1) * 128] for c in range(NC)],
                       [big1[:, c * 1024 + kt * 128:c * 1024 + (kt + 1) * 128]
                        for c in range(NC)],
                       wait_first=lambda t: sems.w(t, "vS"))
        for vt in range(8):
            def ldv2(q, vt=vt):
                dma(q, big0[:, vt * 2048:(vt + 1) * 2048],
                    E["cc4_out"][vro[vt]:vro[vt] + 128, :])
            SY(ldv2)
        for h in range(MH):
            for ps in range(4):
                def mms2(t, h=h, ps=ps):
                    sems.w(t, "vS")
                    t.matmul(pb[2][:, :],
                             big2[:, h * 512 + ps * 128:h * 512 + (ps + 1) * 128],
                             big1[:, h * 1024:h * 1024 + 512],
                             start=True, stop=True)
                    ii = t.matmul(pb[3][:, :],
                                  big2[:, h * 512 + ps * 128:
                                       h * 512 + (ps + 1) * 128],
                                  big1[:, h * 1024 + 512:(h + 1) * 1024],
                                  start=True, stop=True)
                    sems.inc(ii, "tS")
                TE(mms2)

                def ldmk(q, ps=ps):
                    sems.w(q, "vS")
                    dma(q, wf32[:, 1024:2048],
                        E["modmask"][ps * 128:(ps + 1) * 128, :])
                SY(ldmk)

                def evs2(v):
                    sems.w(v, "tS"); sems.w(v, "dS")
                    v.scalar_tensor_tensor(out=wf32[:, 0:512], in0=pb[2][:, :],
                                           scalar=RSQ_HD,
                                           in1=wf32[:, 1024:1536],
                                           op0=ALU.mult, op1=ALU.add)
                    i = v.scalar_tensor_tensor(out=wf32[:, 512:1024],
                                               in0=pb[3][:, :], scalar=RSQ_HD,
                                               in1=wf32[:, 1536:2048],
                                               op0=ALU.mult, op1=ALU.add)
                    c = sems.inc(i, "vS"); sems.w(v, "vS", c)
                    i = v.reduce_max(out=sm[:, 60:61], in_=wf32[:, 0:1024],
                                     axis=AX.X, negate=True)
                    sems.inc(i, "vS")
                VE(evs2)

                def exs2(s):
                    sems.w(s, "vS")
                    ii = s.activation(out=big0[:, 16384:17408],
                                      in_=wf32[:, 0:1024], func=AF.Exp,
                                      bias=sm[:, 60:61],
                                      accum_out=sm[:, 61:62])
                    sems.inc(ii, "sS")
                SC(exs2)

                def nrm2(v):
                    sems.w(v, "sS")
                    i = v.reciprocal(out=sm[:, 62:63], in_=sm[:, 61:62])
                    c = sems.inc(i, "vS"); sems.w(v, "vS", c)
                    i = v.tensor_scalar(out=big0[:, 16384:17408],
                                        in0=big0[:, 16384:17408],
                                        scalar1=sm[:, 62:63], scalar2=None,
                                        op0=ALU.mult)
                    sems.inc(i, "vS")
                VE(nrm2)
                transposes([big0[:, 16384 + kt * 128:16384 + (kt + 1) * 128]
                            for kt in range(8)],
                           [(stg if kt < 4 else stg2)[:, (kt % 4) * 512 +
                                                      ps * 128:(kt % 4) * 512 +
                                                      (ps + 1) * 128]
                            for kt in range(8)],
                           wait_first=lambda t: sems.w(t, "vS"))
            def mmav2(t, h=h):
                sems.w(t, "vS")
                for kt in range(8):
                    ii = t.matmul(pb[4][:, :],
                                  big0[:, kt * 2048 + h * 128:
                                       kt * 2048 + (h + 1) * 128],
                                  (stg if kt < 4 else stg2)[:, (kt % 4) * 512:
                                                            (kt % 4 + 1) * 512],
                                  start=(kt == 0), stop=(kt == 7))
                sems.inc(ii, "tS")
            TE(mmav2)

            def evav2(v, h=h):
                sems.w(v, "tS")
                i = v.tensor_copy(out=big0[:, 18432 + h * 512:
                                           18432 + (h + 1) * 512],
                                  in_=pb[4][:, :])
                sems.inc(i, "vS")
            VE(evav2)

        for oc in range(4):
            def ldwo2(g, oc=oc):
                sems.w(g, "tS")
                g.dma_start(
                    out=big2[:, :].rearrange("p (a d) -> p a d", a=16),
                    in_=E["wo2"][:, oc * 512:(oc + 1) * 512].rearrange(
                        "(a p) d -> p a d", p=128)
                ).then_inc(sems.h["dG"], 16)
                sems.c["dG"] += 16
            GP(ldwo2)
            for ps in range(4):
                def mmo2(t, oc=oc, ps=ps):
                    sems.w(t, "dG"); sems.w(t, "vS")
                    bank = pb[ps % 2]
                    for h in range(MH):
                        ii = t.matmul(bank[:, :],
                                      big0[:, 18432 + h * 512 + ps * 128:
                                           18432 + h * 512 + (ps + 1) * 128],
                                      big2[:, h * 512:(h + 1) * 512],
                                      start=(h == 0), stop=(h == MH - 1))
                    sems.inc(ii, "tS")
                TE(mmo2)

                def evo2(v, oc=oc, ps=ps):
                    sems.w(v, "tS")
                    i = v.tensor_tensor(out=tb4(ps, oc * 512, (oc + 1) * 512),
                                        in0=tb4(ps, oc * 512, (oc + 1) * 512),
                                        in1=pb[ps % 2][:, :], op=ALU.add)
                    sems.inc(i, "vS")
                VE(evo2)

        def g2b(q):
            sems.w(q, "vS")
            dma(q, wf32[0:1, 0:2048], E["g2"][0:1, :])
        SY(g2b)

        def g2c(v):
            sems.w(v, "dS")
            i = v.tensor_copy(out=stg2[0:1, 0:2048], in_=wf32[0:1, 0:2048])
            sems.inc(i, "vS")
        VE(g2c)

        def g2m(t):
            sems.w(t, "vS")
            for j in range(4):
                ii = t.matmul(pb[2 + j][:, :], onesb[0:1, :],
                              stg2[0:1, j * 512:(j + 1) * 512],
                              start=True, stop=True)
            sems.inc(ii, "tS")
        TE(g2m)

        def g2e(v):
            sems.w(v, "tS")
            for j in range(4):
                i = v.tensor_copy(out=wf32[:, j * 512:(j + 1) * 512],
                                  in_=pb[2 + j][:, :])
            sems.inc(i, "vS")
        VE(g2e)
        for ps in range(4):
            ln(tb4(ps), D, 1e-6, tb4(ps), sub_mean=False,
               scratch=stg2[:, 0:2048], g_ap=wf32[:, 0:2048],
               out_bf=stg[:, :])
            transposes([stg[:, c * 128:(c + 1) * 128] for c in range(NC)],
                       [big2[:, c * 512 + ps * 128:c * 512 + (ps + 1) * 128]
                        for c in range(NC)],
                       wait_first=lambda t: sems.w(t, "vS"))
        for fc in range(16):
            def ldw13(g, fc=fc):
                sems.w(g, "tS")
                g.dma_start(
                    out=big1[:, 0:8192].rearrange("p (a d) -> p a d", a=16),
                    in_=E["w1"][:, fc * 512:(fc + 1) * 512].rearrange(
                        "(a p) d -> p a d", p=128)
                ).then_inc(sems.h["dG"], 16)
                g.dma_start(
                    out=big1[:, 8192:16384].rearrange("p (a d) -> p a d", a=16),
                    in_=E["w3"][:, fc * 512:(fc + 1) * 512].rearrange(
                        "(a p) d -> p a d", p=128)
                ).then_inc(sems.h["dG"], 16)
                sems.c["dG"] += 32
            GP(ldw13)
            for ps in range(4):
                def mmup(t, fc=fc, ps=ps):
                    sems.w(t, "dG"); sems.w(t, "vS"); sems.w(t, "sS")
                    for cg in range(16):
                        t.matmul(pb[0][:, :],
                                 big2[:, cg * 512 + ps * 128:
                                      cg * 512 + (ps + 1) * 128],
                                 big1[:, cg * 512:(cg + 1) * 512],
                                 start=(cg == 0), stop=(cg == 15))
                    for cg in range(16):
                        ii = t.matmul(pb[1][:, :],
                                      big2[:, cg * 512 + ps * 128:
                                           cg * 512 + (ps + 1) * 128],
                                      big1[:, 8192 + cg * 512:
                                           8192 + (cg + 1) * 512],
                                      start=(cg == 0), stop=(cg == 15))
                    sems.inc(ii, "tS")
                TE(mmup)

                def evup_s(s):
                    sems.w(s, "tS")
                    ii = s.activation(out=mskb[:, 0:512], in_=pb[0][:, :],
                                      func=AF.Silu)
                    sems.inc(ii, "sS")
                SC(evup_s)

                def evup_v(v, fc=fc, ps=ps):
                    sems.w(v, "sS"); sems.w(v, "tS")
                    i = v.tensor_copy(out=wf32[:, 512:1024], in_=pb[1][:, :])
                    c = sems.inc(i, "vS"); sems.w(v, "vS", c)
                    i = v.tensor_tensor(
                        out=big0[:, ps * 8192 + fc * 512:
                                 ps * 8192 + (fc + 1) * 512],
                        in0=mskb[:, 0:512], in1=wf32[:, 512:1024],
                        op=ALU.mult)
                    sems.inc(i, "vS")
                VE(evup_v)
        for ft in range(64):
            srcs = [big0[:, ps * 8192 + ft * 128:ps * 8192 + (ft + 1) * 128]
                    for ps in range(4)]
            transposes(srcs, [stg[:, ps * 128:(ps + 1) * 128] for ps in range(4)],
                       wait_first=(lambda t: sems.w(t, "vS")) if ft == 0 else
                       (lambda t: sems.w(t, "dS")))

            def stpt(q, ft=ft):
                sems.w(q, "vS")
                dma(q, E["pT_sp"][ft * 128:(ft + 1) * 128, 0:512],
                    stg[:, 0:512])
            SY(stpt)
        mmct2 = [0, 0]
        for oc in range(4):
            for ft in range(64):
                par = ft % 2
                guard = mmct2[par]

                def ldw2c(g, oc=oc, ft=ft, par=par, guard=guard):
                    sems.w(g, "tS", guard)
                    g.dma_start(out=big1[:, par * 512:(par + 1) * 512],
                                in_=E["w2"][ft * 128:(ft + 1) * 128,
                                            oc * 512:(oc + 1) * 512]
                                ).then_inc(sems.h["dG"], 16)
                    sems.c["dG"] += 16
                GP(ldw2c)

                def ldpt2(q, ft=ft, par=par, guard=guard):
                    sems.w(q, "tS", guard)
                    dma(q, big1[:, 1024 + par * 512:1024 + (par + 1) * 512],
                        E["pT_sp"][ft * 128:(ft + 1) * 128, 0:512])
                SY(ldpt2)

                def mmdn(t, ft=ft, par=par):
                    sems.w(t, "dG"); sems.w(t, "dS")
                    if ft == 0:
                        sems.w(t, "vS")
                    for ps in range(4):
                        ii = t.matmul(pb[ps][:, :],
                                      big1[:, 1024 + par * 512 + ps * 128:
                                           1024 + par * 512 + (ps + 1) * 128],
                                      big1[:, par * 512:(par + 1) * 512],
                                      start=(ft == 0), stop=(ft == 63))
                    sems.inc(ii, "tS")
                TE(mmdn)
                mmct2[par] = sems.c["tS"]

            def evdn(v, oc=oc):
                sems.w(v, "tS")
                for ps in range(4):
                    v.tensor_tensor(out=wf32[:, 1024:1536],
                                    in0=tb4(ps, oc * 512, (oc + 1) * 512),
                                    in1=pb[ps][:, :], op=ALU.add)
                    i = v.tensor_scalar(
                        out=big0[:, 16384 + ps * 2048 + oc * 512:
                                 16384 + ps * 2048 + (oc + 1) * 512],
                        in0=wf32[:, 1024:1536], scalar1=sm[:, 52 + ps:53 + ps],
                        scalar2=None, op0=ALU.mult)
                    c = sems.inc(i, "vS"); sems.w(v, "vS", c)
                vnop(v)
            VE(evdn)

        for tt in range(NT):
            for oc in range(4):
                def mmsc(t, tt=tt, oc=oc):
                    sems.w(t, "vS")
                    bank = pb[oc % 2]
                    for pm in range(4):
                        ii = t.matmul(bank[:, :],
                                      big1[:, 16384 + pm * 2048 + tt * 128:
                                           16384 + pm * 2048 + (tt + 1) * 128],
                                      big0[:, 16384 + pm * 2048 + oc * 512:
                                           16384 + pm * 2048 + (oc + 1) * 512],
                                      start=(pm == 0), stop=(pm == 3))
                    sems.inc(ii, "tS")
                TE(mmsc)

                def evsc(v, oc=oc):
                    sems.w(v, "tS")
                    i = v.tensor_copy(out=stg2[:, oc * 512:(oc + 1) * 512],
                                      in_=pb[oc % 2][:, :])
                    sems.inc(i, "vS")
                VE(evsc)

            def stsc(q, tt=tt):
                sems.w(q, "vS")
                dma(q, E["cc5_in"][tt * 128:(tt + 1) * 128, :], stg2[:, :])
            SY(stsc)

            def wsc(v):
                sems.w(v, "dS")
                vnop(v)
            VE(wsc)

        def cc5(g):
            sems.w(g, "dS")
            ii = g.collective_compute(
                "ReduceScatter", ALU.add, replica_groups=RG_PAIRS,
                ins=[E["cc5_in"][:, :]], outs=[E["cc5_out"][:, :]])
            sems.inc(ii, "ccS")
        GP(cc5)

        def wcc5(v):
            sems.w(v, "ccS")
            vnop(v)
        VE(wcc5)
        for tt in range(NTO):
            def ldf(q, tt=tt):
                sems.w(q, "vS")
                dma(q, stg2[:, :], E["cc5_out"][tt * 128:(tt + 1) * 128, :])
                dma(q, xt32[:, :], E["x3_sp"][tt * 128:(tt + 1) * 128, :])
            SY(ldf)

            def adf(v):
                sems.w(v, "dS")
                i = v.tensor_copy(out=wf32[:, 0:2048], in_=stg2[:, :])
                c = sems.inc(i, "vS"); sems.w(v, "vS", c)
                i = v.scalar_tensor_tensor(out=st32[:, :], in0=xt32[:, :],
                                           scalar=2.0, in1=wf32[:, 0:2048],
                                           op0=ALU.mult, op1=ALU.add)
                sems.inc(i, "vS")
            VE(adf)
            ln(st32[:, :], D, 1e-5, st32[:, :], scratch=acc32[:, 0:2048])

            def stf(q, tt=tt):
                sems.w(q, "vS")
                dma(q, E["out"][tt * 128:(tt + 1) * 128, :], st32[:, :])
            SY(stf)

            def wstf(v):
                sems.w(v, "dS")
                vnop(v)
            VE(wstf)

        def finb4(q):
            sems.w(q, "dS"); sems.w(q, "dG")
        SY(finb4)

    def p_finish(src_name):
        for tt in range(NTO):
            def ld(q, tt=tt):
                sems.w(q, "vS")
                dma(q, xt32[:, :], E[src_name][tt * 128:(tt + 1) * 128, :])
            SY(ld)

            def cp(v):
                sems.w(v, "dS")
                iv = v.tensor_copy(out=st32[:, :], in_=xt32[:, :])
                sems.inc(iv, "vS")
            VE(cp)

            def stq(q, tt=tt):
                sems.w(q, "vS")
                dma(q, E["out"][tt * 128:(tt + 1) * 128, :], st32[:, :])
            SY(stq)

        def fin(q):
            sems.w(q, "dS")
            sems.w(q, "dG")
        SY(fin)

    p_const()
    p_block1()
    if nblocks == 1:
        p_finish("x1_sp")
    if nblocks >= 2:
        p_block2()
    if nblocks == 2:
        p_finish("x2_sp")
    if nblocks >= 3:
        p_block3(cut=b3cut)
    if nblocks == 3:
        p_finish("x3_sp" if b3cut > 75 else "x2_sp")
    if nblocks >= 4:
        p_block4()

    blk.__exit__(None, None, None)
    for cm in reversed(ctxs):
        cm.__exit__(None, None, None)
    for cm in sems.cms:
        cm.__exit__(None, None, None)
    return nc


def kernel(**inputs):
    shards = stage_inputs(inputs)
    nc = build(nblocks=4)
    res = run_bass_kernel_spmd(nc, shards, core_ids=list(range(8)))
    out = np.empty((B, S, D), np.float32)
    for s in range(8):
        pair, half = s // 2, s % 2
        out[pair, half * HALF:(half + 1) * HALF, :] = res.results[s]["out"]
    return out
